# version 03
# baseline (speedup 1.0000x reference)
import contextlib
import numpy as np
import ml_dtypes
import concourse.bass as bass
import concourse.mybir as mybir
from concourse.bass_utils import run_bass_kernel_spmd

F32 = mybir.dt.float32
I32 = mybir.dt.int32
BF16 = mybir.dt.bfloat16
ALU = mybir.AluOpType
ACT = mybir.ActivationFunctionType
AXL = mybir.AxisListType

T, C = 2048, 1024
NT, NCH, NH, HD = 16, 8, 16, 64
QCW, NQC, NKT = 512, 4, 16
IFF, NIT = 2816, 22
EPS = 1e-5

STAGE = 4
ATT_SUB = 5


class _Sem:
    __slots__ = ("h", "n")

    def __init__(self, handle):
        self.h = handle
        self.n = 0


class _Plan:

    def __init__(self, real_block):
        self.rb = real_block
        self.q = {k: [] for k in ("sync", "tensor", "vector", "scalar", "gpsimd")}

    def sync(self, f):
        self.q["sync"].append(f)

    def tensor(self, f):
        self.q["tensor"].append(f)

    def vector(self, f):
        self.q["vector"].append(f)

    def scalar(self, f):
        self.q["scalar"].append(f)

    def gpsimd(self, f):
        self.q["gpsimd"].append(f)

    def flush(self):
        for name in ("sync", "tensor", "vector", "scalar", "gpsimd"):
            fns = self.q[name]
            if not fns:
                continue
            self.q[name] = []

            def _run(e, fns=fns):
                for f in fns:
                    f(e)
            getattr(self.rb, name)(_run)


def _build(stage):
    nc = bass.Bass("TRN2", target_bir_lowering=False, debug=False)

    def din(name, shape, dt=F32):
        return nc.dram_tensor(name, list(shape), dt, kind="ExternalInput").ap()

    d = {"x": din("x", [T, C])}
    for L in ("l0", "l1"):
        for w in ("wq", "wk", "wv", "wo"):
            d[f"{L}_{w}"] = din(f"{L}_{w}", [C, C])
        for b in ("bq", "bk", "bv", "bo"):
            d[f"{L}_{b}"] = din(f"{L}_{b}", [C])
    for w in ("wr", "wk", "wv"):
        d[f"f0_{w}"] = din(f"f0_{w}", [C, C])
    d["f0_mur"] = din("f0_mur", [C])
    d["f0_muk"] = din("f0_muk", [C])
    d["f1_wg"] = din("f1_wg", [IFF, C])
    d["f1_wu"] = din("f1_wu", [IFF, C])
    d["f1_wd"] = din("f1_wd", [C, IFF])
    d["c_ident"] = din("c_ident", [128, 128], BF16)
    d["c_tri"] = din("c_tri", [128, 128], BF16)
    d["c_alibi"] = din("c_alibi", [128, NH * NKT], F32)
    out_ap = nc.dram_tensor("out", [T, C], F32, kind="ExternalOutput").ap()
    hq_dram = nc.dram_tensor("hq_scr", [NT, NIT, 128, 128], BF16,
                             kind="Internal").ap()
    rec_dram = nc.dram_tensor("rec_scr", [16, 512], BF16, kind="Internal").ap()

    with contextlib.ExitStack() as est:
        def sbuf(name, shape, dt):
            return est.enter_context(nc.sbuf_tensor(name, list(shape), dt))

        def psum(name, shape, dt):
            return est.enter_context(nc.psum_tensor(name, list(shape), dt))

        xbuf = sbuf("xbuf", [128, NT, C], F32)
        qTa = sbuf("qTa", [128, NCH, T], BF16)
        kTa = sbuf("kTa", [128, NCH * T], BF16)
        Va = sbuf("Va", [128, NT * C], BF16)
        wcur = sbuf("wcur", [128, NCH, C], BF16)
        u1 = sbuf("u1", [128, NCH * QCW], BF16)
        u2 = sbuf("u2", [128, C], BF16)
        ctxq = sbuf("ctxq", [128, NCH * QCW], BF16)
        scrB = sbuf("scrB", [128, 2592], F32)
        recb = sbuf("recb", [128, 512], BF16)
        epsc = sbuf("epsc", [128, 1], F32)
        den16 = scrB[:, 1536:2048]
        stats = sbuf("stats", [128, 16], F32)
        bq_c = sbuf("bq_c", [128, 8], F32)
        bk_c = sbuf("bk_c", [128, 8], F32)
        mur_c = sbuf("mur_c", [128, 8], F32)
        muk_c = sbuf("muk_c", [128, 8], F32)
        brow = sbuf("brow", [1, C], BF16)
        ident = sbuf("ident", [128, 128], BF16)
        tri = sbuf("tri", [128, 128], BF16)

        ps_s = [psum("ps_s0", [128, 512], F32), psum("ps_s1", [128, 512], F32)]
        ps_x = [psum("ps_x0", [128, 512], F32), psum("ps_x1", [128, 512], F32)]
        ps_w = psum("ps_w", [128, 1024], F32)
        ps_e = psum("ps_e", [128, 512], F32)
        ps_tr = psum("ps_tr", [128, 1024], BF16)

        LD = _Sem(est.enter_context(nc.semaphore("s_ld")))
        GL = _Sem(est.enter_context(nc.semaphore("s_gl")))
        DN = _Sem(est.enter_context(nc.semaphore("s_dn")))
        HQ = _Sem(est.enter_context(nc.semaphore("s_hq")))
        OUT = _Sem(est.enter_context(nc.semaphore("s_out")))
        TE = _Sem(est.enter_context(nc.semaphore("s_te")))
        VE = _Sem(est.enter_context(nc.semaphore("s_ve")))
        SE = _Sem(est.enter_context(nc.semaphore("s_se")))
        GE = _Sem(est.enter_context(nc.semaphore("s_ge")))
        BAR = _Sem(est.enter_context(nc.semaphore("s_bar")))

        rblock = est.enter_context(nc.Block())
        block = _Plan(rblock)

        def bar():
            block.flush()
            BAR.n += 5
            tgt = BAR.n

            def _f(e, tgt=tgt):
                e.drain()
                e.nop().then_inc(BAR.h, 1)
                e.wait_ge(BAR.h, tgt)
            for em in (rblock.sync, rblock.tensor, rblock.vector, rblock.scalar,
                       rblock.gpsimd):
                em(_f)

        def p0():
            def _y(e):
                xv = xbuf[:]
                dv = d["x"].rearrange("(t p) c -> p t c", p=128)
                for g in range(4):
                    e.dma_start(xv[:, g * 4:(g + 1) * 4],
                                dv[:, g * 4:(g + 1) * 4]).then_inc(LD.h, 16)
                for ap, key in ((ident[:], "c_ident"), (tri[:], "c_tri")):
                    e.dma_start(ap, d[key]).then_inc(LD.h, 16)
            LD.n += 96
            block.sync(_y)

            GL.n += 64
            GE.n += 1
            gl_t0 = GL.n

            def _g(e, gl_t0=gl_t0):
                for ap, key in ((bq_c[:], "l0_bq"), (bk_c[:], "l0_bk"),
                                (mur_c[:], "f0_mur"), (muk_c[:], "f0_muk")):
                    e.dma_start(ap, d[key].rearrange("(c p) -> p c", p=128),
                            allow_slow_non_contiguous=True).then_inc(GL.h, 16)
                e.wait_ge(GL.h, gl_t0)
                e.memset(epsc[:], float(EPS))
                e.drain()
                e.nop().then_inc(GE.h, 1)
            block.gpsimd(_g)
            ld_t0 = LD.n
            ge_t0 = GE.n

            def _v(e, ld_t0=ld_t0, ge_t0=ge_t0):
                e.wait_ge(LD.h, ld_t0)
                e.wait_ge(GE.h, ge_t0)
            block.vector(_v)
            bar()

        def load_brow(key):
            GL.n += 16

            def _g(e):
                e.dma_start(brow[:, :], d[key].rearrange("(a c) -> a c", a=1))\
                    .then_inc(GL.h, 16)
            block.gpsimd(_g)
            tgt = GL.n

            def _t(e, tgt=tgt):
                e.wait_ge(GL.h, tgt)
            block.tensor(_t)

        def ln_tile(t, dst, src=None):
            xin = xbuf[:, t, :] if src is None else src
            se0 = SE.n
            VE.n += 1

            def _v(e, xin=xin, se0=se0):
                e.wait_ge(SE.h, se0)
                e.bn_stats(stats[:, 0:6], xin[:, 0:512])
                e.bn_stats(stats[:, 6:12], xin[:, 512:1024])
                e.drain()
                e.bn_aggr(stats[:, 12:14], stats[:, 0:12])
                e.drain()
                e.nop().then_inc(VE.h, 1)
            block.vector(_v)
            ve1 = VE.n
            SE.n += 1

            def _s(e, ve1=ve1):
                e.wait_ge(VE.h, ve1)
                e.activation(stats[:, 14:15], stats[:, 13:14], ACT.Sqrt,
                             bias=epsc[:, 0:1])
                e.drain()
                e.nop().then_inc(SE.h, 1)
            block.scalar(_s)
            se1 = SE.n
            VE.n += 1

            def _v2(e, se1=se1):
                e.wait_ge(SE.h, se1)
                e.reciprocal(stats[:, 14:15], stats[:, 14:15])
                e.drain()
                e.scalar_tensor_tensor(stats[:, 15:16], stats[:, 12:13], -1.0,
                                       stats[:, 14:15], ALU.mult, ALU.mult)
                e.drain()
                e.nop().then_inc(VE.h, 1)
            block.vector(_v2)
            ve2 = VE.n
            SE.n += 1

            def _s2(e, xin=xin, dst=dst, ve2=ve2):
                e.wait_ge(VE.h, ve2)
                e.activation(dst, xin, ACT.Identity,
                             bias=stats[:, 15:16], scale=stats[:, 14:15])
                e.drain()
                e.nop().then_inc(SE.h, 1)
            block.scalar(_s2)

        def transpose_tile(src, dst_fn, ncc=8):
            for g in range((ncc + 3) // 4):
                kk = min(4, ncc - g * 4)
                se_t = SE.n
                ve_t = VE.n
                t0 = TE.n

                TE.n += 1

                def _t(e, g=g, kk=kk, se_t=se_t, ve_t=ve_t, src=src):
                    e.wait_ge(SE.h, se_t)
                    e.wait_ge(VE.h, ve_t)
                    li = None
                    for k in range(kk):
                        cc = g * 4 + k
                        li = e.transpose(ps_tr[:, k * 128:(k + 1) * 128],
                                         src[:, cc * 128:(cc + 1) * 128], ident[:])
                    li.then_inc(TE.h, 1)
                block.tensor(_t)
                VE.n += 1

                def _v(e, g=g, kk=kk, t0=t0, dst_fn=dst_fn):
                    e.wait_ge(TE.h, t0 + 1)
                    for k in range(kk):
                        e.tensor_copy(dst_fn(g * 4 + k),
                                      ps_tr[:, k * 128:(k + 1) * 128])
                    e.drain()
                    e.nop().then_inc(VE.h, 1)
                block.vector(_v)

        def wprep_T(src, wdst_fn):
            for r in range(8):
                ve_t = VE.n

                GL.n += 16

                def _g(e, r=r, ve_t=ve_t, src=src):
                    e.wait_ge(VE.h, ve_t)
                    e.dma_start(u2[:, :], src[r * 128:(r + 1) * 128, :])\
                        .then_inc(GL.h, 16)
                block.gpsimd(_g)
                gl_t = GL.n
                transpose_strip_w(gl_t, wdst_fn, r)

        def transpose_strip_w(gl_t, wdst_fn, r):
            for g in range(2):
                ve_t = VE.n
                t0 = TE.n

                TE.n += 1

                def _t(e, g=g, gl_t=gl_t, ve_t=ve_t):
                    e.wait_ge(GL.h, gl_t)
                    e.wait_ge(VE.h, ve_t)
                    li = None
                    for k in range(4):
                        cc = g * 4 + k
                        li = e.transpose(ps_tr[:, k * 128:(k + 1) * 128],
                                         u2[:, cc * 128:(cc + 1) * 128], ident[:])
                    li.then_inc(TE.h, 1)
                block.tensor(_t)
                VE.n += 1

                def _v(e, g=g, t0=t0, r=r, wdst_fn=wdst_fn):
                    e.wait_ge(TE.h, t0 + 1)
                    for k in range(4):
                        cc = g * 4 + k
                        e.tensor_copy(wdst_fn(cc)[:, r * 128:(r + 1) * 128],
                                      ps_tr[:, k * 128:(k + 1) * 128])
                    e.drain()
                    e.nop().then_inc(VE.h, 1)
                block.vector(_v)

        def wload_nat(src, wdst, nr=NCH):
            GL.n += 16

            def _g(e, src=src, wdst=wdst, nr=nr):
                e.dma_start(wdst[:, 0:nr, :],
                            src.rearrange("(r p) c -> p r c", p=128))\
                    .then_inc(GL.h, 16)
            block.gpsimd(_g)

        def quant(src, dst, w, base):
            nb = w // 16
            u_a = scrB[:, base:base + w]
            u_t = scrB[:, base + w:base + 2 * w]
            u_c = scrB[:, base + 2 * w:base + 3 * w]
            u_s = u_t
            m_ = scrB[:, base + 2 * w:base + 2 * w + nb]
            scl = scrB[:, base + 2 * w + nb:base + 2 * w + 2 * nb]
            rb_ = scrB[:, base + 2 * w + 2 * nb:base + 2 * w + 3 * nb]
            sb2 = scrB[:, base + 3 * w:base + 3 * w + nb]

            def _v(e):
                sv = src.rearrange("p (b k) -> p b k", k=16)
                e.tensor_reduce(m_, sv, AXL.X, ALU.max, apply_absolute_value=True)
                e.drain()
                e.tensor_scalar(scl, m_, 1.0 / 6.0, 1e-12, ALU.mult, ALU.max)
                e.drain()
                e.reciprocal(rb_, scl)
                e.tensor_scalar(sb2, scl, 0.5, None, ALU.mult)
                e.drain()
                e.tensor_reduce(u_a, src.rearrange("p (n o) -> p n o", o=1),
                                AXL.X, ALU.max, apply_absolute_value=True)
                e.drain()
                rbb = rb_.unsqueeze(2).broadcast_to([128, nb, 16])
                e.tensor_tensor(u_t.rearrange("p (b k) -> p b k", k=16),
                                u_a.rearrange("p (b k) -> p b k", k=16), rbb,
                                ALU.mult)
                e.drain()
                e.tensor_scalar(u_c, u_t, 5.0, 2.0, ALU.is_gt, ALU.mult)
                e.tensor_scalar(u_a, u_t, 0.25, None, ALU.is_gt)
                e.drain()
                e.scalar_tensor_tensor(u_c, u_t, 3.5, u_c, ALU.is_gt, ALU.add)
                e.scalar_tensor_tensor(u_a, u_t, 0.75, u_a, ALU.is_gt, ALU.add)
                e.drain()
                e.scalar_tensor_tensor(u_c, u_t, 2.5, u_c, ALU.is_gt, ALU.add)
                e.scalar_tensor_tensor(u_a, u_t, 1.25, u_a, ALU.is_gt, ALU.add)
                e.drain()
                e.scalar_tensor_tensor(u_a, u_t, 1.75, u_a, ALU.is_gt, ALU.add)
                e.drain()
                e.scalar_tensor_tensor(u_a, u_c, 2.0, u_a, ALU.mult, ALU.add)
                e.drain()
                sbb = sb2.unsqueeze(2).broadcast_to([128, nb, 16])
                e.tensor_tensor(u_c.rearrange("p (b k) -> p b k", k=16),
                                u_a.rearrange("p (b k) -> p b k", k=16), sbb,
                                ALU.mult)
                e.drain()
                e.tensor_scalar(u_s, src, 0.0, 2.0, ALU.is_ge, ALU.mult)
                e.drain()
                e.scalar_tensor_tensor(dst, u_s, -1.0, u_c, ALU.add, ALU.mult)
                e.drain()
                e.nop().then_inc(VE.h, 1)
            VE.n += 1
            block.vector(_v)

        def emit_out():
            ve_t = VE.n
            se_t = SE.n
            OUT.n += 64
            out_t = OUT.n

            def _y(e, ve_t=ve_t, se_t=se_t, out_t=out_t):
                e.wait_ge(VE.h, ve_t)
                e.wait_ge(SE.h, se_t)
                xv = xbuf[:]
                dv = out_ap.rearrange("(t p) c -> p t c", p=128)
                for g in range(4):
                    e.dma_start(dv[:, g * 4:(g + 1) * 4],
                                xv[:, g * 4:(g + 1) * 4]).then_inc(OUT.h, 16)
                e.wait_ge(OUT.h, out_t)
            block.sync(_y)
            block.flush()

        def attention(layer):
            l = f"l{layer}"
            causal = (layer == 0)
            trw = (layer == 0)
            lnTb = u1[:].rearrange("p (c q) -> p c q", q=QCW)

            if layer == 1:
                GL.n += 32
                gl_b = GL.n

                def _g(e, gl_b=gl_b):
                    e.dma_start(bq_c[:], d["l1_bq"].rearrange("(c p) -> p c", p=128), allow_slow_non_contiguous=True).then_inc(GL.h, 16)
                    e.dma_start(bk_c[:], d["l1_bk"].rearrange("(c p) -> p c", p=128), allow_slow_non_contiguous=True).then_inc(GL.h, 16)
                    e.wait_ge(GL.h, gl_b)
                block.gpsimd(_g)
                bar()

            va3 = Va[:].rearrange("p (t c) -> p t c", c=C)
            prods = ("k", "v", "q")[:ATT_SUB]
            for which in prods:
                if which == "v":
                    load_brow(f"{l}_bv")
                if trw:
                    wprep_T(d[f"{l}_w{which}"], lambda cc: wcur[:, cc, :])
                else:
                    wload_nat(d[f"{l}_w{which}"], wcur)
                bar()
                for qc in range(NQC):
                    for tt in range(4):
                        t = qc * 4 + tt
                        ln_tile(t, u2[:, :])
                        transpose_tile(u2[:, :],
                                       lambda cc, tt=tt: lnTb[:, cc, tt * 128:(tt + 1) * 128])
                    bar()
                    if which in ("k", "q"):
                        dst = kTa[:].rearrange("p (c q) -> p c q", q=T) if which == "k" \
                            else qTa[:, :, :]
                        bcol = bk_c if which == "k" else bq_c
                        evac_tgts = []
                        for m in range(8):
                            t0 = TE.n
                            se_need = evac_tgts[m - 2] if m >= 2 else SE.n

                            def _t(e, m=m, se_need=se_need):
                                e.wait_ge(SE.h, se_need)
                                li = None
                                for cc in range(8):
                                    li = e.matmul(ps_s[m % 2][:, :],
                                                  wcur[:, cc, m * 128:(m + 1) * 128],
                                                  lnTb[:, cc, :],
                                                  start=(cc == 0), stop=(cc == 7))
                                li.then_inc(TE.h, 1)
                            block.tensor(_t)
                            TE.n += 1

                            def _s(e, m=m, qc=qc, t0=t0, dst=dst, bcol=bcol):
                                e.wait_ge(TE.h, t0 + 1)
                                e.activation(dst[:, m, qc * QCW:(qc + 1) * QCW],
                                             ps_s[m % 2][:, :], ACT.Identity,
                                             bias=bcol[:, m:m + 1])
                                e.drain()
                                e.nop().then_inc(SE.h, 1)
                            block.scalar(_s)
                            SE.n += 1
                            evac_tgts.append(SE.n)
                    else:
                        evac_tgts = []
                        for tt in range(4):
                            t = qc * 4 + tt
                            t0 = TE.n
                            ve_need = evac_tgts[tt - 1] if tt >= 1 else VE.n

                            def _t(e, tt=tt, ve_need=ve_need):
                                e.wait_ge(VE.h, ve_need)
                                li = None
                                for half in range(2):
                                    for cc in range(8):
                                        e.matmul(ps_w[:, half * 512:(half + 1) * 512],
                                                 lnTb[:, cc, tt * 128:(tt + 1) * 128],
                                                 wcur[:, cc, half * 512:(half + 1) * 512],
                                                 start=(cc == 0), stop=False)
                                    li = e.matmul(ps_w[:, half * 512:(half + 1) * 512],
                                                  tri[0:1, :],
                                                  brow[:, half * 512:(half + 1) * 512],
                                                  start=False, stop=True)
                                li.then_inc(TE.h, 1)
                            block.tensor(_t)
                            TE.n += 1

                            def _v(e, t=t, t0=t0):
                                e.wait_ge(TE.h, t0 + 1)
                                e.tensor_copy(va3[:, t, :], ps_w[:, :])
                                e.drain()
                                e.nop().then_inc(VE.h, 1)
                            block.vector(_v)
                            VE.n += 1
                            evac_tgts.append(VE.n)
                bar()

            if ATT_SUB < 4:
                return
            load_brow(f"{l}_bo")
            if trw:
                wprep_T(d[f"{l}_wo"], lambda cc: wcur[:, cc, :])
            else:
                wload_nat(d[f"{l}_wo"], wcur)
            bar()

            kT3 = kTa[:].rearrange("p (c q) -> p c q", q=T)
            cq = ctxq[:].rearrange("p (c q) -> p c q", q=QCW)
            rec_bf = scrB[32:48, 1024:1280].bitcast(BF16)
            alibi = u1[:, 3072:3584].bitcast(F32)
            if layer == 1:
                LD.n += 16
                al_t = LD.n

                def _yal(e):
                    e.dma_start(alibi, d["c_alibi"]).then_inc(LD.h, 16)
                block.sync(_yal)

                def _sal(e, al_t=al_t):
                    e.wait_ge(LD.h, al_t)
                block.scalar(_sal)
            for qc in range(NQC):
                kts = list(range(4 * qc + 4)) if causal else list(range(NKT))
                exp_done = {}
                mask_done = {}
                av_done = {}
                ctx_evac = {}
                den_evac = {}
                den_dma = {}
                for hi, h in enumerate(range(NH)):
                    p0_ = (h % 2) * 64
                    chk = h // 2
                    xpar = h % 2
                    dps = ps_e if (h % 2 == 0) else ps_w
                    for ki, kt in enumerate(kts):
                        spar = ki % 2
                        seqi = hi * len(kts) + ki
                        w_exp = exp_done.get(seqi - 2, 0)

                        def _t(e, h=h, kt=kt, spar=spar, p0_=p0_, chk=chk,
                               qc=qc, w_exp=w_exp):
                            e.wait_ge(SE.h, w_exp)
                            li = e.matmul(ps_s[spar][:, :],
                                          kT3[p0_:p0_ + 64, chk, kt * 128:(kt + 1) * 128],
                                          qTa[p0_:p0_ + 64, chk, qc * QCW:(qc + 1) * QCW],
                                          start=True, stop=True)
                            li.then_inc(TE.h, 1)
                        block.tensor(_t)
                        TE.n += 1
                        sc_t = TE.n
                        w_av2 = av_done.get(seqi - 2, 0)

                        def _s(e, h=h, kt=kt, spar=spar, sc_t=sc_t,
                               layer=layer, w_av2=w_av2):
                            e.wait_ge(TE.h, max(sc_t, w_av2))
                            if layer == 1:
                                b = alibi[:, h * NKT + kt:h * NKT + kt + 1]
                                e.activation(u2[:, spar * 512:(spar + 1) * 512],
                                             ps_s[spar][:, :], ACT.Exp,
                                             bias=b, scale=0.125)
                            else:
                                e.activation(u2[:, spar * 512:(spar + 1) * 512],
                                             ps_s[spar][:, :], ACT.Exp,
                                             scale=0.125)
                            e.drain()
                            e.nop().then_inc(SE.h, 1)
                        block.scalar(_s)
                        SE.n += 1
                        exp_done[seqi] = SE.n

                        need_mask = causal and (4 * qc <= kt <= 4 * qc + 3)
                        if need_mask:
                            v = kt - 4 * qc
                            se_t = SE.n

                            def _v(e, spar=spar, v=v, se_t=se_t):
                                e.wait_ge(SE.h, se_t)
                                if v > 0:
                                    e.memset(u2[:, spar * 512:spar * 512 + 128 * v], 0.0)
                                e.tensor_tensor(
                                    u2[:, spar * 512 + 128 * v:spar * 512 + 128 * (v + 1)],
                                    u2[:, spar * 512 + 128 * v:spar * 512 + 128 * (v + 1)],
                                    tri[:], ALU.mult)
                                e.drain()
                                e.nop().then_inc(VE.h, 1)
                            block.vector(_v)
                            VE.n += 1
                            mask_done[seqi] = VE.n

                        se_t = SE.n
                        ve_t = mask_done.get(seqi, None)
                        ve_ctx = ctx_evac.get(hi - 2, 0)
                        ve_den = den_evac.get(hi - 2, 0)

                        def _t2(e, h=h, kt=kt, spar=spar, xpar=xpar, p0_=p0_,
                                se_t=se_t, ve_t=ve_t, ve_ctx=ve_ctx,
                                ve_den=ve_den, dps=dps,
                                first=(ki == 0), last=(ki == len(kts) - 1)):
                            e.wait_ge(SE.h, se_t)
                            if ve_t is not None:
                                e.wait_ge(VE.h, ve_t)
                            if first:
                                e.wait_ge(VE.h, max(ve_ctx, ve_den))
                            e.matmul(ps_x[xpar][p0_:p0_ + 64, :],
                                     va3[:, kt, h * 64:(h + 1) * 64],
                                     u2[:, spar * 512:(spar + 1) * 512],
                                     start=first, stop=last)
                            li = e.matmul(dps[0:1, 0:512],
                                          tri[:, 127:128],
                                          u2[:, spar * 512:(spar + 1) * 512],
                                          start=first, stop=last)
                            li.then_inc(TE.h, 1)
                        block.tensor(_t2)
                        TE.n += 1
                        av_done[seqi] = TE.n

                    av_t = TE.n
                    dn_prev = den_dma.get(hi - 2, 0)

                    def _v(e, h=h, xpar=xpar, p0_=p0_, chk=chk, av_t=av_t,
                           dn_prev=dn_prev, dps=dps):
                        e.wait_ge(TE.h, av_t)
                        e.wait_ge(DN.h, dn_prev)
                        e.tensor_copy(cq[p0_:p0_ + 64, chk, :],
                                      ps_x[xpar][p0_:p0_ + 64, :])
                        e.drain()
                        e.nop().then_inc(VE.h, 1)
                        e.tensor_copy(scrB[0:1, xpar * 512:(xpar + 1) * 512],
                                      dps[0:1, 0:512])
                        e.drain()
                        e.nop().then_inc(VE.h, 1)
                    block.vector(_v)
                    VE.n += 1
                    VE.n += 1
                    ctx_evac[hi] = VE.n
                    den_evac[hi] = VE.n
                    ve_t = VE.n

                    def _y(e, h=h, xpar=xpar, ve_t=ve_t):
                        e.wait_ge(VE.h, ve_t)
                        e.dma_start(den16[h:h + 1, :],
                                    scrB[0:1, xpar * 512:(xpar + 1) * 512])\
                            .then_inc(DN.h, 16)
                    block.sync(_y)
                    DN.n += 16
                    den_dma[hi] = DN.n

                dn_t = DN.n

                def _vr(e, dn_t=dn_t):
                    e.wait_ge(DN.h, dn_t)
                    e.reciprocal(den16[0:16, :], den16[0:16, :])
                    e.drain()
                    e.tensor_copy(rec_bf, den16[0:16, :])
                    e.drain()
                    e.nop().then_inc(VE.h, 1)
                block.vector(_vr)
                VE.n += 1
                ve_rec = VE.n

                def _yrec(e, ve_rec=ve_rec):
                    e.wait_ge(VE.h, ve_rec)
                    e.dma_start(rec_dram, rec_bf).then_inc(DN.h, 16)
                block.sync(_yrec)
                DN.n += 16

                for chk in range(8):
                    ve_t = VE.n
                    rec_t = DN.n

                    def _y(e, chk=chk, ve_t=ve_t, rec_t=rec_t):
                        e.wait_ge(VE.h, ve_t)
                        e.wait_ge(DN.h, rec_t)
                        s0 = rec_dram[2 * chk, :].unsqueeze(0).broadcast_to([64, 512])
                        s1 = rec_dram[2 * chk + 1, :].unsqueeze(0).broadcast_to([64, 512])
                        e.dma_start(recb[0:64, :], s0).then_inc(DN.h, 16)
                        e.dma_start(recb[64:128, :], s1).then_inc(DN.h, 16)
                    block.sync(_y)
                    DN.n += 32
                    dn_t = DN.n

                    def _v2(e, chk=chk, dn_t=dn_t):
                        e.wait_ge(DN.h, dn_t)
                        e.tensor_tensor(cq[:, chk, :], cq[:, chk, :], recb[:, :],
                                        ALU.mult)
                        e.drain()
                        e.nop().then_inc(VE.h, 1)
                    block.vector(_v2)
                    VE.n += 1
                norm_all = VE.n

                for half in range(2 if ATT_SUB >= 5 else 0):
                    ve_free = VE.n
                    t0 = TE.n

                    def _t3(e, half=half, qc=qc, ve_free=ve_free):
                        e.wait_ge(VE.h, ve_free)
                        for ti in range(2):
                            t_loc = half * 2 + ti
                            for nh2 in range(2):
                                pp = (ps_s if ti == 0 else ps_x)[nh2]
                                for jc in range(8):
                                    e.matmul(pp[:, :],
                                             cq[:, jc, t_loc * 128:(t_loc + 1) * 128],
                                             wcur[:, jc, nh2 * 512:(nh2 + 1) * 512],
                                             start=(jc == 0), stop=False)
                                li = e.matmul(pp[:, :], tri[0:1, :],
                                              brow[:, nh2 * 512:(nh2 + 1) * 512],
                                              start=False, stop=True)
                        li.then_inc(TE.h, 1)
                    block.tensor(_t3)
                    TE.n += 1
                    op_t = TE.n

                    def _v3(e, half=half, qc=qc, op_t=op_t):
                        e.wait_ge(TE.h, op_t)
                        for ti in range(2):
                            t = qc * 4 + half * 2 + ti
                            for nh2 in range(2):
                                pp = (ps_s if ti == 0 else ps_x)[nh2]
                                e.tensor_tensor(xbuf[:, t, nh2 * 512:(nh2 + 1) * 512],
                                                xbuf[:, t, nh2 * 512:(nh2 + 1) * 512],
                                                pp[:, :], ALU.add)
                        e.drain()
                        e.nop().then_inc(VE.h, 1)
                    block.vector(_v3)
                    VE.n += 1
                bar()

        def ffn0():
            kf = kTa[:]
            xr = kf[:, 0:4096].rearrange("p (c q) -> p c q", q=512)
            xk = kf[:, 4096:8192].rearrange("p (c q) -> p c q", q=512)
            kact = kf[:, 8192:12288].rearrange("p (c q) -> p c q", q=512)
            Vf = Va[:]
            wkf = Vf[:, 0:8192].rearrange("p (c k) -> p c k", k=C)
            wvf = Vf[:, 8192:16384].rearrange("p (c k) -> p c k", k=C)
            ln2T = qTa

            wprep_T(d["f0_wr"], lambda cc: wcur[:, cc, :])
            wprep_T(d["f0_wk"], lambda cc: wkf[:, cc, :])
            wprep_T(d["f0_wv"], lambda cc: wvf[:, cc, :])
            bar()
            for t in range(NT):
                ln_tile(t, u2[:, :])
                transpose_tile(u2[:, :],
                               lambda cc, t=t: ln2T[:, cc, t * 128:(t + 1) * 128])
            bar()

            dbuf = scrB[:, 0:512]
            rfull = scrB[:, 512:1536]
            for pc in range(NQC):
                s0 = pc * 512
                for cc in range(8):
                    def _v(e, cc=cc, s0=s0, pc=pc):
                        if pc == 0:
                            cur = ln2T[:, cc, 1:512]
                            prv = ln2T[:, cc, 0:511]
                            e.tensor_sub(dbuf[:, 1:512], cur, prv)
                            e.drain()
                            e.scalar_tensor_tensor(xr[:, cc, 1:512], dbuf[:, 1:512],
                                                   mur_c[:, cc:cc + 1], prv,
                                                   ALU.mult, ALU.add)
                            e.scalar_tensor_tensor(xk[:, cc, 1:512], dbuf[:, 1:512],
                                                   muk_c[:, cc:cc + 1], prv,
                                                   ALU.mult, ALU.add)
                            e.drain()
                            e.tensor_scalar(xr[:, cc, 0:1], ln2T[:, cc, 0:1],
                                            mur_c[:, cc:cc + 1], None, ALU.mult)
                            e.tensor_scalar(xk[:, cc, 0:1], ln2T[:, cc, 0:1],
                                            muk_c[:, cc:cc + 1], None, ALU.mult)
                            e.drain()
                        else:
                            cur = ln2T[:, cc, s0:s0 + 512]
                            prv = ln2T[:, cc, s0 - 1:s0 + 511]
                            e.tensor_sub(dbuf[:, :], cur, prv)
                            e.drain()
                            e.scalar_tensor_tensor(xr[:, cc, :], dbuf[:, :],
                                                   mur_c[:, cc:cc + 1], prv,
                                                   ALU.mult, ALU.add)
                            e.scalar_tensor_tensor(xk[:, cc, :], dbuf[:, :],
                                                   muk_c[:, cc:cc + 1], prv,
                                                   ALU.mult, ALU.add)
                            e.drain()
                        e.nop().then_inc(VE.h, 1)
                    block.vector(_v)
                    VE.n += 1
                xrk_done = VE.n

                se_evt = []
                for m in range(8):
                    t0 = TE.n
                    se_prev = se_evt[m - 2] if m >= 2 else SE.n

                    def _t(e, m=m, se_prev=se_prev, xrk=xrk_done):
                        e.wait_ge(VE.h, xrk)
                        e.wait_ge(SE.h, se_prev)
                        li = None
                        for cc in range(8):
                            li = e.matmul(ps_s[m % 2][:, :],
                                          wkf[:, cc, m * 128:(m + 1) * 128],
                                          xk[:, cc, :], start=(cc == 0), stop=(cc == 7))
                        li.then_inc(TE.h, 1)
                    block.tensor(_t)
                    TE.n += 1

                    def _s(e, m=m, t0=t0):
                        e.wait_ge(TE.h, t0 + 1)
                        e.activation(u2[:, (m % 2) * 512:(m % 2) * 512 + 512],
                                     ps_s[m % 2][:, :], ACT.Relu)
                        e.drain()
                        e.nop().then_inc(SE.h, 1)
                    block.scalar(_s)
                    SE.n += 1
                    se_evt.append(SE.n)
                    se_now = SE.n

                    def _v(e, m=m, se_now=se_now):
                        e.wait_ge(SE.h, se_now)
                        rk = u2[:, (m % 2) * 512:(m % 2) * 512 + 512]
                        e.tensor_tensor(kact[:, m, :], rk, rk, ALU.mult)
                        e.drain()
                        e.nop().then_inc(VE.h, 1)
                    block.vector(_v)
                    VE.n += 1
                kact_done = VE.n

                for tt in range(4):
                    t = pc * 4 + tt
                    t0 = TE.n
                    ve_now = VE.n

                    def _t(e, tt=tt, kd=kact_done, ve_now=ve_now):
                        e.wait_ge(VE.h, max(kd, ve_now))
                        li = None
                        for half in range(2):
                            for cc in range(8):
                                li = e.matmul(ps_w[:, half * 512:(half + 1) * 512],
                                              xr[:, cc, tt * 128:(tt + 1) * 128],
                                              wcur[:, cc, half * 512:(half + 1) * 512],
                                              start=(cc == 0), stop=(cc == 7))
                        li.then_inc(TE.h, 1)
                    block.tensor(_t)
                    TE.n += 1
                    r_t = TE.n

                    def _s(e, r_t=r_t):
                        e.wait_ge(TE.h, r_t)
                        e.activation(rfull[:, :], ps_w[:, :], ACT.Sigmoid)
                        e.drain()
                        e.nop().then_inc(SE.h, 1)
                    block.scalar(_s)
                    SE.n += 1
                    sg_t = SE.n

                    def _t2(e, tt=tt, sg_t=sg_t):
                        e.wait_ge(SE.h, sg_t)
                        li = None
                        for half in range(2):
                            for m in range(8):
                                li = e.matmul(ps_w[:, half * 512:(half + 1) * 512],
                                              kact[:, m, tt * 128:(tt + 1) * 128],
                                              wvf[:, m, half * 512:(half + 1) * 512],
                                              start=(m == 0), stop=(m == 7))
                        li.then_inc(TE.h, 1)
                    block.tensor(_t2)
                    TE.n += 1
                    ov_t = TE.n

                    def _v(e, t=t, ov_t=ov_t):
                        e.wait_ge(TE.h, ov_t)
                        e.tensor_tensor(rfull[:, :], rfull[:, :], ps_w[:, :],
                                        ALU.mult)
                        e.drain()
                        e.tensor_tensor(xbuf[:, t, :], xbuf[:, t, :], rfull[:, :],
                                        ALU.add)
                        e.drain()
                        e.nop().then_inc(VE.h, 1)
                    block.vector(_v)
                    VE.n += 1
            bar()

        def ffn1():
            xqT = qTa
            kf = kTa[:]
            for t in range(NT):
                ln_tile(t, scrB[:, 0:1024])
                ve_t = VE.n
                se_t = SE.n

                def _v0(e, se_t=se_t):
                    e.wait_ge(SE.h, se_t)
                block.vector(_v0)
                for sub in range(2):
                    quant(scrB[:, sub * 512:(sub + 1) * 512],
                          u2[:, sub * 512:(sub + 1) * 512], 512, 1024)
                transpose_tile(u2[:, :],
                               lambda cc, t=t: xqT[:, cc, t * 128:(t + 1) * 128])
            bar()

            wg_sl = [kf[:, s * 2048:(s + 1) * 2048].rearrange("p (c k) -> p c k", k=256)
                     for s in range(2)]
            wu_sl = [kf[:, (2 + s) * 2048:(3 + s) * 2048].rearrange("p (c k) -> p c k", k=256)
                     for s in range(2)]
            hqw = [kf[:, 8192 + s * 1024:8192 + (s + 1) * 1024] for s in range(2)]
            hbuf = scrB[:, 0:256]

            def prep_wslice(src, dstsl, ic):
                for rt in range(2):
                    ve_t = VE.n

                    def _y(e, rt=rt, ic=ic, ve_t=ve_t):
                        e.wait_ge(VE.h, ve_t)
                        e.dma_start(scrB[:, 0:1024].bitcast(F32) if False else scrB[:, 0:1024],
                                    src[ic * 256 + rt * 128:ic * 256 + (rt + 1) * 128, :])\
                            .then_inc(LD.h, 16)
                    block.sync(_y)
                    LD.n += 16
                    ld_t = LD.n

                    def _v(e, ld_t=ld_t):
                        e.wait_ge(LD.h, ld_t)
                    block.vector(_v)
                    for sub in range(2):
                        quant(scrB[:, sub * 512:(sub + 1) * 512],
                              u2[:, sub * 512:(sub + 1) * 512], 512, 1024)
                    transpose_tile(u2[:, :],
                                   lambda cc, rt=rt: dstsl[:, cc, rt * 128:(rt + 1) * 128])

            NIC = IFF // 256
            hq_dmas = {}
            for ic in range(NIC):
                sl = ic % 2
                prep_wslice(d["f1_wg"], wg_sl[sl], ic)
                prep_wslice(d["f1_wu"], wu_sl[sl], ic)
                bar()
                for t in range(NT):
                    t0 = TE.n
                    se_t = SE.n
                    ve_t = VE.n

                    def _t(e, t=t, sl=sl, se_t=se_t, ve_t=ve_t):
                        e.wait_ge(SE.h, se_t)
                        e.wait_ge(VE.h, ve_t)
                        li = None
                        for cc in range(8):
                            e.matmul(ps_s[0][:, 0:256],
                                     xqT[:, cc, t * 128:(t + 1) * 128],
                                     wg_sl[sl][:, cc, :], start=(cc == 0), stop=(cc == 7))
                        for cc in range(8):
                            li = e.matmul(ps_s[1][:, 0:256],
                                          xqT[:, cc, t * 128:(t + 1) * 128],
                                          wu_sl[sl][:, cc, :], start=(cc == 0), stop=(cc == 7))
                        li.then_inc(TE.h, 1)
                    block.tensor(_t)
                    TE.n += 1
                    mm_t = TE.n

                    def _s(e, mm_t=mm_t):
                        e.wait_ge(TE.h, mm_t)
                        e.activation(hbuf[:, :], ps_s[0][:, 0:256], ACT.Silu)
                        e.drain()
                        e.nop().then_inc(SE.h, 1)
                    block.scalar(_s)
                    SE.n += 1
                    si_t = SE.n

                    def _v(e, si_t=si_t):
                        e.wait_ge(SE.h, si_t)
                        e.tensor_tensor(hbuf[:, :], hbuf[:, :], ps_s[1][:, 0:256],
                                        ALU.mult)
                        e.drain()
                        e.nop().then_inc(VE.h, 1)
                    block.vector(_v)
                    VE.n += 1
                    quant(hbuf[:, :], u2[:, 0:256], 256, 1024)
                    hs = t % 2
                    hq_prev = hq_dmas.get(t - 2, 0)

                    def _vw(e, hq_prev=hq_prev):
                        e.wait_ge(HQ.h, hq_prev)
                    block.vector(_vw)
                    transpose_tile(u2[:, 0:256],
                                   lambda b, hs=hs: hqw[hs][:, b * 128:(b + 1) * 128],
                                   ncc=2)
                    ve_t2 = VE.n

                    def _y(e, t=t, ic=ic, hs=hs, ve_t2=ve_t2):
                        e.wait_ge(VE.h, ve_t2)
                        e.dma_start(hq_dram[t, ic * 2:(ic + 1) * 2].rearrange("b p q -> p b q"),
                                    hqw[hs][:, 0:256].rearrange("p (b q) -> p b q", q=128))\
                            .then_inc(HQ.h, 16)
                    block.sync(_y)
                    HQ.n += 16
                    hq_dmas[t] = HQ.n
            bar()

            wd_chunk = []
            Vf = Va[:]
            cqf = ctxq[:]
            u1f = u1[:]
            for kk in range(NIT):
                if kk < 16:
                    wd_chunk.append(Vf[:, kk * 1024:(kk + 1) * 1024])
                elif kk < 20:
                    wd_chunk.append(cqf[:, (kk - 16) * 1024:(kk - 15) * 1024])
                else:
                    wd_chunk.append(u1f[:, (kk - 20) * 1024:(kk - 19) * 1024])

            for rt in range(8):
                for piece, pw in ((0, 1024), (1, 1024), (2, 768)):
                    off = piece * 1024
                    ve_t = VE.n

                    def _y(e, rt=rt, off=off, pw=pw, ve_t=ve_t):
                        e.wait_ge(VE.h, ve_t)
                        e.dma_start(scrB[:, 0:pw],
                                    d["f1_wd"][rt * 128:(rt + 1) * 128, off:off + pw])\
                            .then_inc(LD.h, 16)
                    block.sync(_y)
                    LD.n += 16
                    ld_t = LD.n

                    def _v(e, ld_t=ld_t):
                        e.wait_ge(LD.h, ld_t)
                    block.vector(_v)
                    offs = [(0, 512), (512, 512)] if pw == 1024 else [(0, 512), (512, 256)]
                    for so, sw in offs:
                        quant(scrB[:, so:so + sw], u2[:, so:so + sw], sw, 1024)
                    nb128 = pw // 128
                    transpose_tile(u2[:, 0:pw],
                                   lambda b, rt=rt, off=off: wd_chunk[(off // 128) + b][:, rt * 128:(rt + 1) * 128],
                                   ncc=nb128)
            bar()

            hq_sl = [kf[:, s * 2816:(s + 1) * 2816] for s in range(2)]
            use_done = {}
            for t in range(NT):
                sl = t % 2
                prev_use = use_done.get(t - 2, 0)

                def _y(e, t=t, sl=sl, prev_use=prev_use):
                    e.wait_ge(TE.h, prev_use)
                    e.dma_start(hq_sl[sl][:].rearrange("p (b q) -> p b q", q=128),
                                hq_dram[t].rearrange("b p q -> p b q")).then_inc(HQ.h, 16)
                block.sync(_y)
                HQ.n += 16
                hq_t = HQ.n
                ve_t = VE.n

                def _t(e, t=t, sl=sl, hq_t=hq_t, ve_t=ve_t):
                    e.wait_ge(HQ.h, hq_t)
                    e.wait_ge(VE.h, ve_t)
                    li = None
                    for half in range(2):
                        for kk in range(NIT):
                            li = e.matmul(ps_w[:, half * 512:(half + 1) * 512],
                                          hq_sl[sl][:, kk * 128:(kk + 1) * 128],
                                          wd_chunk[kk][:, half * 512:(half + 1) * 512],
                                          start=(kk == 0), stop=(kk == NIT - 1))
                    li.then_inc(TE.h, 1)
                block.tensor(_t)
                TE.n += 1
                use_done[t] = TE.n
                mm_t = TE.n

                def _v(e, t=t, mm_t=mm_t):
                    e.wait_ge(TE.h, mm_t)
                    e.tensor_tensor(xbuf[:, t, :], xbuf[:, t, :], ps_w[:, :],
                                    ALU.add)
                    e.drain()
                    e.nop().then_inc(VE.h, 1)
                block.vector(_v)
                VE.n += 1
            bar()

        p0()
        if stage >= 1:
            attention(0)
        if stage >= 2:
            ffn0()
        if stage >= 3:
            attention(1)
        if stage >= 4:
            ffn1()
        emit_out()

    return nc


def _consts():
    ident = np.eye(128, dtype=np.float32).astype(ml_dtypes.bfloat16)
    kp = np.arange(128)[:, None]
    qp = np.arange(128)[None, :]
    tri = (qp >= kp).astype(np.float32).astype(ml_dtypes.bfloat16)
    alibi = np.zeros((128, NH * NKT), dtype=np.float32)
    xs = (2.0 ** 8) ** (1.0 / NH)
    slopes = np.array([1.0 / xs ** (i + 1) for i in range(NH)], dtype=np.float64)
    p = np.arange(128)
    for h in range(NH):
        for kt in range(NKT):
            alibi[:, h * NKT + kt] = (slopes[h] * (kt * 128 + p - (T - 1))).astype(np.float32)
    return ident, tri, alibi


def kernel(**inputs):
    B = 8
    ident, tri, alibi = _consts()
    nc = _build(STAGE)
    base = {
        "l0_wq": inputs["L0_attn_Wq"], "l0_wk": inputs["L0_attn_Wk"],
        "l0_wv": inputs["L0_attn_Wv"], "l0_wo": inputs["L0_attn_Wo"],
        "l0_bq": inputs["L0_attn_bq"], "l0_bk": inputs["L0_attn_bk"],
        "l0_bv": inputs["L0_attn_bv"], "l0_bo": inputs["L0_attn_bo"],
        "f0_wr": inputs["L0_ffn_W_r"], "f0_wk": inputs["L0_ffn_W_k"],
        "f0_wv": inputs["L0_ffn_W_v"],
        "f0_mur": inputs["L0_ffn_mu_r"], "f0_muk": inputs["L0_ffn_mu_k"],
        "l1_wq": inputs["L1_attn_Wq"], "l1_wk": inputs["L1_attn_Wk"],
        "l1_wv": inputs["L1_attn_Wv"], "l1_wo": inputs["L1_attn_Wo"],
        "l1_bq": inputs["L1_attn_bq"], "l1_bk": inputs["L1_attn_bk"],
        "l1_bv": inputs["L1_attn_bv"], "l1_bo": inputs["L1_attn_bo"],
        "f1_wg": inputs["L1_ffn_w_gate"], "f1_wu": inputs["L1_ffn_w_up"],
        "f1_wd": inputs["L1_ffn_w_down"],
        "c_ident": ident, "c_tri": tri, "c_alibi": alibi,
    }
    base = {k: np.ascontiguousarray(np.asarray(v)) for k, v in base.items()}
    in_maps = []
    for b in range(B):
        m = dict(base)
        m["x"] = np.ascontiguousarray(np.asarray(inputs["L0_attn_x"][b]))
        in_maps.append(m)
    res = run_bass_kernel_spmd(nc, in_maps, core_ids=list(range(B)))
    out = np.stack([np.asarray(res.results[b]["out"]) for b in range(B)], axis=0)
    return out.astype(np.float32)


# version 04
# speedup vs baseline: 1.1064x; 1.1064x over previous
import contextlib
import numpy as np
import ml_dtypes
import concourse.bass as bass
import concourse.mybir as mybir
from concourse.bass_utils import run_bass_kernel_spmd

F32 = mybir.dt.float32
I32 = mybir.dt.int32
BF16 = mybir.dt.bfloat16
ALU = mybir.AluOpType
ACT = mybir.ActivationFunctionType
AXL = mybir.AxisListType

T, C = 2048, 1024
NT, NCH, NH, HD = 16, 8, 16, 64
QCW, NQC, NKT = 512, 4, 16
IFF, NIT = 2816, 22
EPS = 1e-5

STAGE = 4
ATT_SUB = 5


class _Sem:
    __slots__ = ("h", "n")

    def __init__(self, handle):
        self.h = handle
        self.n = 0


class _Plan:

    def __init__(self, real_block):
        self.rb = real_block
        self.q = {k: [] for k in ("sync", "tensor", "vector", "scalar", "gpsimd")}

    def sync(self, f):
        self.q["sync"].append(f)

    def tensor(self, f):
        self.q["tensor"].append(f)

    def vector(self, f):
        self.q["vector"].append(f)

    def scalar(self, f):
        self.q["scalar"].append(f)

    def gpsimd(self, f):
        self.q["gpsimd"].append(f)

    def flush(self):
        for name in ("sync", "tensor", "vector", "scalar", "gpsimd"):
            fns = self.q[name]
            if not fns:
                continue
            self.q[name] = []

            def _run(e, fns=fns):
                for f in fns:
                    f(e)
            getattr(self.rb, name)(_run)


def _build(stage):
    nc = bass.Bass("TRN2", target_bir_lowering=False, debug=False)

    def din(name, shape, dt=F32):
        return nc.dram_tensor(name, list(shape), dt, kind="ExternalInput").ap()

    d = {"x": din("x", [T, C])}
    for L in ("l0", "l1"):
        for w in ("wq", "wk", "wv", "wo"):
            d[f"{L}_{w}"] = din(f"{L}_{w}", [C, C])
        for b in ("bq", "bk", "bv", "bo"):
            d[f"{L}_{b}"] = din(f"{L}_{b}", [C])
    for w in ("wr", "wk", "wv"):
        d[f"f0_{w}"] = din(f"f0_{w}", [C, C])
    d["f0_mur"] = din("f0_mur", [C])
    d["f0_muk"] = din("f0_muk", [C])
    d["f1_wg_sl"] = din("f1_wg_sl", [IFF // 8, C])
    d["f1_wu_sl"] = din("f1_wu_sl", [IFF // 8, C])
    d["f1_wd_sl"] = din("f1_wd_sl", [C // 8, IFF])
    d["c_ident"] = din("c_ident", [128, 128], BF16)
    d["c_tri"] = din("c_tri", [128, 128], BF16)
    d["c_alibi"] = din("c_alibi", [128, NH * NKT], F32)
    out_ap = nc.dram_tensor("out", [T, C], F32, kind="ExternalOutput").ap()
    hq_dram = nc.dram_tensor("hq_scr", [NT, NIT, 128, 128], BF16,
                             kind="Internal").ap()
    rec_dram = nc.dram_tensor("rec_scr", [16, 512], BF16, kind="Internal").ap()
    ccg_in = nc.dram_tensor("ccg_in", [IFF // 8, C], BF16, kind="Internal").ap()
    ccu_in = nc.dram_tensor("ccu_in", [IFF // 8, C], BF16, kind="Internal").ap()
    ccd_in = nc.dram_tensor("ccd_in", [C // 8, IFF], BF16, kind="Internal").ap()
    ccg = nc.dram_tensor("ccg", [IFF, C], BF16, kind="Internal").ap()
    ccu = nc.dram_tensor("ccu", [IFF, C], BF16, kind="Internal").ap()
    ccd = nc.dram_tensor("ccd", [C, IFF], BF16, kind="Internal").ap()

    with contextlib.ExitStack() as est:
        def sbuf(name, shape, dt):
            return est.enter_context(nc.sbuf_tensor(name, list(shape), dt))

        def psum(name, shape, dt):
            return est.enter_context(nc.psum_tensor(name, list(shape), dt))

        xbuf = sbuf("xbuf", [128, NT, C], F32)
        qTa = sbuf("qTa", [128, NCH, T], BF16)
        kTa = sbuf("kTa", [128, NCH * T], BF16)
        Va = sbuf("Va", [128, NT * C], BF16)
        wcur = sbuf("wcur", [128, NCH, C], BF16)
        u1 = sbuf("u1", [128, NCH * QCW], BF16)
        u2 = sbuf("u2", [128, C], BF16)
        ctxq = sbuf("ctxq", [128, NCH * QCW], BF16)
        scrB = sbuf("scrB", [128, 2592], F32)
        recb = sbuf("recb", [128, 512], BF16)
        epsc = sbuf("epsc", [128, 1], F32)
        den16 = scrB[:, 1536:2048]
        stats = sbuf("stats", [128, 16], F32)
        bq_c = sbuf("bq_c", [128, 8], F32)
        bk_c = sbuf("bk_c", [128, 8], F32)
        mur_c = sbuf("mur_c", [128, 8], F32)
        muk_c = sbuf("muk_c", [128, 8], F32)
        brow = sbuf("brow", [1, C], BF16)
        ident = sbuf("ident", [128, 128], BF16)
        tri = sbuf("tri", [128, 128], BF16)

        ps_s = [psum("ps_s0", [128, 512], F32), psum("ps_s1", [128, 512], F32)]
        ps_x = [psum("ps_x0", [128, 512], F32), psum("ps_x1", [128, 512], F32)]
        ps_w = psum("ps_w", [128, 1024], F32)
        ps_e = psum("ps_e", [128, 512], F32)
        ps_tr = psum("ps_tr", [128, 1024], BF16)

        LD = _Sem(est.enter_context(nc.semaphore("s_ld")))
        GL = _Sem(est.enter_context(nc.semaphore("s_gl")))
        DN = _Sem(est.enter_context(nc.semaphore("s_dn")))
        HQ = _Sem(est.enter_context(nc.semaphore("s_hq")))
        OUT = _Sem(est.enter_context(nc.semaphore("s_out")))
        TE = _Sem(est.enter_context(nc.semaphore("s_te")))
        VE = _Sem(est.enter_context(nc.semaphore("s_ve")))
        SE = _Sem(est.enter_context(nc.semaphore("s_se")))
        GE = _Sem(est.enter_context(nc.semaphore("s_ge")))
        CC = _Sem(est.enter_context(nc.semaphore("s_cc")))
        BAR = _Sem(est.enter_context(nc.semaphore("s_bar")))

        rblock = est.enter_context(nc.Block())
        block = _Plan(rblock)

        def bar():
            block.flush()
            BAR.n += 5
            tgt = BAR.n

            def _f(e, tgt=tgt):
                e.drain()
                e.nop().then_inc(BAR.h, 1)
                e.wait_ge(BAR.h, tgt)
            for em in (rblock.sync, rblock.tensor, rblock.vector, rblock.scalar,
                       rblock.gpsimd):
                em(_f)

        def p0():
            def _y(e):
                xv = xbuf[:]
                dv = d["x"].rearrange("(t p) c -> p t c", p=128)
                for g in range(4):
                    e.dma_start(xv[:, g * 4:(g + 1) * 4],
                                dv[:, g * 4:(g + 1) * 4]).then_inc(LD.h, 16)
                for ap, key in ((ident[:], "c_ident"), (tri[:], "c_tri")):
                    e.dma_start(ap, d[key]).then_inc(LD.h, 16)
            LD.n += 96
            block.sync(_y)

            GL.n += 64
            GE.n += 1
            gl_t0 = GL.n

            def _g(e, gl_t0=gl_t0):
                for ap, key in ((bq_c[:], "l0_bq"), (bk_c[:], "l0_bk"),
                                (mur_c[:], "f0_mur"), (muk_c[:], "f0_muk")):
                    e.dma_start(ap, d[key].rearrange("(c p) -> p c", p=128),
                            allow_slow_non_contiguous=True).then_inc(GL.h, 16)
                e.wait_ge(GL.h, gl_t0)
                e.memset(epsc[:], float(EPS))
                e.drain()
                e.nop().then_inc(GE.h, 1)
            block.gpsimd(_g)
            ld_t0 = LD.n
            ge_t0 = GE.n

            def _v(e, ld_t0=ld_t0, ge_t0=ge_t0):
                e.wait_ge(LD.h, ld_t0)
                e.wait_ge(GE.h, ge_t0)
            block.vector(_v)
            bar()

        def load_brow(key):
            GL.n += 16

            def _g(e):
                e.dma_start(brow[:, :], d[key].rearrange("(a c) -> a c", a=1))\
                    .then_inc(GL.h, 16)
            block.gpsimd(_g)
            tgt = GL.n

            def _t(e, tgt=tgt):
                e.wait_ge(GL.h, tgt)
            block.tensor(_t)

        def ln_tile(t, dst, src=None):
            xin = xbuf[:, t, :] if src is None else src
            se0 = SE.n
            VE.n += 1

            def _v(e, xin=xin, se0=se0):
                e.wait_ge(SE.h, se0)
                e.bn_stats(stats[:, 0:6], xin[:, 0:512])
                e.bn_stats(stats[:, 6:12], xin[:, 512:1024])
                e.drain()
                e.bn_aggr(stats[:, 12:14], stats[:, 0:12])
                e.drain()
                e.nop().then_inc(VE.h, 1)
            block.vector(_v)
            ve1 = VE.n
            SE.n += 1

            def _s(e, ve1=ve1):
                e.wait_ge(VE.h, ve1)
                e.activation(stats[:, 14:15], stats[:, 13:14], ACT.Sqrt,
                             bias=epsc[:, 0:1])
                e.drain()
                e.nop().then_inc(SE.h, 1)
            block.scalar(_s)
            se1 = SE.n
            VE.n += 1

            def _v2(e, se1=se1):
                e.wait_ge(SE.h, se1)
                e.reciprocal(stats[:, 14:15], stats[:, 14:15])
                e.drain()
                e.scalar_tensor_tensor(stats[:, 15:16], stats[:, 12:13], -1.0,
                                       stats[:, 14:15], ALU.mult, ALU.mult)
                e.drain()
                e.nop().then_inc(VE.h, 1)
            block.vector(_v2)
            ve2 = VE.n
            SE.n += 1

            def _s2(e, xin=xin, dst=dst, ve2=ve2):
                e.wait_ge(VE.h, ve2)
                e.activation(dst, xin, ACT.Identity,
                             bias=stats[:, 15:16], scale=stats[:, 14:15])
                e.drain()
                e.nop().then_inc(SE.h, 1)
            block.scalar(_s2)

        def transpose_tile(src, dst_fn, ncc=8, ge_wait=None, te_guard=None):
            for g in range((ncc + 3) // 4):
                kk = min(4, ncc - g * 4)
                se_t = SE.n
                ve_t = VE.n
                t0 = TE.n

                TE.n += 1

                def _t(e, g=g, kk=kk, se_t=se_t, ve_t=ve_t, src=src,
                       ge_wait=ge_wait):
                    e.wait_ge(SE.h, se_t)
                    e.wait_ge(VE.h, ve_t)
                    if ge_wait is not None:
                        e.wait_ge(GE.h, ge_wait)
                    li = None
                    for k in range(kk):
                        cc = g * 4 + k
                        li = e.transpose(ps_tr[:, k * 128:(k + 1) * 128],
                                         src[:, cc * 128:(cc + 1) * 128], ident[:])
                    li.then_inc(TE.h, 1)
                block.tensor(_t)
                VE.n += 1

                def _v(e, g=g, kk=kk, t0=t0, dst_fn=dst_fn, te_guard=te_guard):
                    e.wait_ge(TE.h, t0 + 1)
                    if te_guard is not None:
                        e.wait_ge(TE.h, te_guard)
                    for k in range(kk):
                        e.tensor_copy(dst_fn(g * 4 + k),
                                      ps_tr[:, k * 128:(k + 1) * 128])
                    e.drain()
                    e.nop().then_inc(VE.h, 1)
                block.vector(_v)

        def wprep_T(src, wdst_fn):
            for r in range(8):
                ve_t = VE.n

                GL.n += 16

                def _g(e, r=r, ve_t=ve_t, src=src):
                    e.wait_ge(VE.h, ve_t)
                    e.dma_start(u2[:, :], src[r * 128:(r + 1) * 128, :])\
                        .then_inc(GL.h, 16)
                block.gpsimd(_g)
                gl_t = GL.n
                transpose_strip_w(gl_t, wdst_fn, r)

        def transpose_strip_w(gl_t, wdst_fn, r):
            for g in range(2):
                ve_t = VE.n
                t0 = TE.n

                TE.n += 1

                def _t(e, g=g, gl_t=gl_t, ve_t=ve_t):
                    e.wait_ge(GL.h, gl_t)
                    e.wait_ge(VE.h, ve_t)
                    li = None
                    for k in range(4):
                        cc = g * 4 + k
                        li = e.transpose(ps_tr[:, k * 128:(k + 1) * 128],
                                         u2[:, cc * 128:(cc + 1) * 128], ident[:])
                    li.then_inc(TE.h, 1)
                block.tensor(_t)
                VE.n += 1

                def _v(e, g=g, t0=t0, r=r, wdst_fn=wdst_fn):
                    e.wait_ge(TE.h, t0 + 1)
                    for k in range(4):
                        cc = g * 4 + k
                        e.tensor_copy(wdst_fn(cc)[:, r * 128:(r + 1) * 128],
                                      ps_tr[:, k * 128:(k + 1) * 128])
                    e.drain()
                    e.nop().then_inc(VE.h, 1)
                block.vector(_v)

        def wload_nat(src, wdst, nr=NCH):
            GL.n += 16

            def _g(e, src=src, wdst=wdst, nr=nr):
                e.dma_start(wdst[:, 0:nr, :],
                            src.rearrange("(r p) c -> p r c", p=128))\
                    .then_inc(GL.h, 16)
            block.gpsimd(_g)

        def _quant_gps(src, dst, w, nb, u_a, u_t, u_c, u_s, m_, scl, rb_, sb2):
            t1 = u_c[:, 128:128 + w // 2]
            t2 = u_c[:, 128 + w // 2:128 + w // 2 + w // 4]
            t3 = u_c[:, 128 + w // 2 - 64:128 + w // 2 - 64 + w // 8]

            def _g1(e):
                e.tensor_scalar(u_a.bitcast(I32), src.bitcast(I32), 0x7FFFFFFF,
                                None, ALU.bitwise_and)
                e.drain()
                a2 = u_a.rearrange("p (b k) -> p b k", k=2)
                e.tensor_tensor(t1, a2[:, :, 0], a2[:, :, 1], ALU.max)
                e.drain()
                b2 = t1.rearrange("p (b k) -> p b k", k=2)
                e.tensor_tensor(t2, b2[:, :, 0], b2[:, :, 1], ALU.max)
                e.drain()
                c2 = t2.rearrange("p (b k) -> p b k", k=2)
                e.tensor_tensor(t3, c2[:, :, 0], c2[:, :, 1], ALU.max)
                e.drain()
                d2 = t3.rearrange("p (b k) -> p b k", k=2)
                e.tensor_tensor(m_, d2[:, :, 0], d2[:, :, 1], ALU.max)
                e.drain()
                e.tensor_scalar(scl, m_, 1.0 / 6.0, 1e-12, ALU.mult, ALU.max)
                e.drain()
                e.tensor_scalar(sb2, scl, 0.5, None, ALU.mult)
                e.drain()
                e.nop().then_inc(GE.h, 1)
            GE.n += 1
            block.gpsimd(_g1)
            ge1 = GE.n
            VE.n += 1

            def _vr(e, ge1=ge1):
                e.wait_ge(GE.h, ge1)
                e.reciprocal(rb_, scl)
                e.drain()
                e.nop().then_inc(VE.h, 1)
            block.vector(_vr)
            ve1 = VE.n
            GE.n += 1

            def _g2(e, ve1=ve1):
                e.wait_ge(VE.h, ve1)
                rbb = rb_.unsqueeze(2).broadcast_to([pp, nb, 16])
                e.tensor_tensor(u_t.rearrange("p (b k) -> p b k", k=16),
                                u_a.rearrange("p (b k) -> p b k", k=16), rbb,
                                ALU.mult)
                e.drain()
                e.tensor_scalar(u_c, u_t, 5.0, 2.0, ALU.is_gt, ALU.mult)
                e.tensor_scalar(u_a, u_t, 0.25, None, ALU.is_gt)
                e.drain()
                e.scalar_tensor_tensor(u_c, u_t, 3.5, u_c, ALU.is_gt, ALU.add)
                e.scalar_tensor_tensor(u_a, u_t, 0.75, u_a, ALU.is_gt, ALU.add)
                e.drain()
                e.scalar_tensor_tensor(u_c, u_t, 2.5, u_c, ALU.is_gt, ALU.add)
                e.scalar_tensor_tensor(u_a, u_t, 1.25, u_a, ALU.is_gt, ALU.add)
                e.drain()
                e.scalar_tensor_tensor(u_a, u_t, 1.75, u_a, ALU.is_gt, ALU.add)
                e.drain()
                e.scalar_tensor_tensor(u_a, u_c, 2.0, u_a, ALU.mult, ALU.add)
                e.drain()
                sbb = sb2.unsqueeze(2).broadcast_to([pp, nb, 16])
                e.tensor_tensor(u_c.rearrange("p (b k) -> p b k", k=16),
                                u_a.rearrange("p (b k) -> p b k", k=16), sbb,
                                ALU.mult)
                e.drain()
                e.tensor_scalar(u_s, src, 0.0, 2.0, ALU.is_ge, ALU.mult)
                e.drain()
                e.scalar_tensor_tensor(dst, u_s, -1.0, u_c, ALU.add, ALU.mult)
                e.drain()
                e.nop().then_inc(GE.h, 1)
            block.gpsimd(_g2)

        def quant(src, dst, w, base, eng="dve", scr=None, pp=128):
            nb = w // 16
            S_ = (scrB if scr is None else scr)[0:pp, :]
            u_a = S_[:, base:base + w]
            u_t = S_[:, base + w:base + 2 * w]
            u_c = S_[:, base + 2 * w:base + 3 * w]
            u_s = u_t
            m_ = S_[:, base + 2 * w:base + 2 * w + nb]
            scl = S_[:, base + 2 * w + nb:base + 2 * w + 2 * nb]
            rb_ = S_[:, base + 2 * w + 2 * nb:base + 2 * w + 3 * nb]
            if eng == "gps":
                sb2 = recb[0:pp].bitcast(F32)[:, 0:nb]
                return _quant_gps(src, dst, w, nb, u_a, u_t, u_c, u_s,
                                  m_, scl, rb_, sb2)
            sb2 = S_[:, base + 3 * w:base + 3 * w + nb]

            def _v(e):
                sv = src.rearrange("p (b k) -> p b k", k=16)
                e.tensor_reduce(m_, sv, AXL.X, ALU.max, apply_absolute_value=True)
                e.drain()
                e.tensor_scalar(scl, m_, 1.0 / 6.0, 1e-12, ALU.mult, ALU.max)
                e.drain()
                e.reciprocal(rb_, scl)
                e.tensor_scalar(sb2, scl, 0.5, None, ALU.mult)
                e.drain()
                e.tensor_reduce(u_a, src.rearrange("p (n o) -> p n o", o=1),
                                AXL.X, ALU.max, apply_absolute_value=True)
                e.drain()
                rbb = rb_.unsqueeze(2).broadcast_to([pp, nb, 16])
                e.tensor_tensor(u_t.rearrange("p (b k) -> p b k", k=16),
                                u_a.rearrange("p (b k) -> p b k", k=16), rbb,
                                ALU.mult)
                e.drain()
                e.tensor_scalar(u_c, u_t, 5.0, 2.0, ALU.is_gt, ALU.mult)
                e.tensor_scalar(u_a, u_t, 0.25, None, ALU.is_gt)
                e.drain()
                e.scalar_tensor_tensor(u_c, u_t, 3.5, u_c, ALU.is_gt, ALU.add)
                e.scalar_tensor_tensor(u_a, u_t, 0.75, u_a, ALU.is_gt, ALU.add)
                e.drain()
                e.scalar_tensor_tensor(u_c, u_t, 2.5, u_c, ALU.is_gt, ALU.add)
                e.scalar_tensor_tensor(u_a, u_t, 1.25, u_a, ALU.is_gt, ALU.add)
                e.drain()
                e.scalar_tensor_tensor(u_a, u_t, 1.75, u_a, ALU.is_gt, ALU.add)
                e.drain()
                e.scalar_tensor_tensor(u_a, u_c, 2.0, u_a, ALU.mult, ALU.add)
                e.drain()
                sbb = sb2.unsqueeze(2).broadcast_to([pp, nb, 16])
                e.tensor_tensor(u_c.rearrange("p (b k) -> p b k", k=16),
                                u_a.rearrange("p (b k) -> p b k", k=16), sbb,
                                ALU.mult)
                e.drain()
                e.tensor_scalar(u_s, src, 0.0, 2.0, ALU.is_ge, ALU.mult)
                e.drain()
                e.scalar_tensor_tensor(dst, u_s, -1.0, u_c, ALU.add, ALU.mult)
                e.drain()
                e.nop().then_inc(VE.h, 1)
            VE.n += 1
            block.vector(_v)

        def emit_out():
            ve_t = VE.n
            se_t = SE.n
            OUT.n += 64
            out_t = OUT.n

            def _y(e, ve_t=ve_t, se_t=se_t, out_t=out_t):
                e.wait_ge(VE.h, ve_t)
                e.wait_ge(SE.h, se_t)
                xv = xbuf[:]
                dv = out_ap.rearrange("(t p) c -> p t c", p=128)
                for g in range(4):
                    e.dma_start(dv[:, g * 4:(g + 1) * 4],
                                xv[:, g * 4:(g + 1) * 4]).then_inc(OUT.h, 16)
                e.wait_ge(OUT.h, out_t)
            block.sync(_y)
            block.flush()

        def ffn1_weights_cc():
            def qslice(src_rows, n_rows, ncols, cc_in):
                row_tiles = []
                r0 = 0
                while r0 < n_rows:
                    rr = min(128, n_rows - r0)
                    row_tiles.append((r0, rr))
                    r0 += rr
                for (r0, rr) in row_tiles:
                    for p0c in range(0, ncols, 1024):
                        pw = min(1024, ncols - p0c)
                        ve_t = VE.n
                        LD.n += 16

                        def _y(e, r0=r0, rr=rr, p0c=p0c, pw=pw, ve_t=ve_t,
                               src_rows=src_rows):
                            e.wait_ge(VE.h, ve_t)
                            e.dma_start(scrB[0:rr, 0:pw],
                                        src_rows[r0:r0 + rr, p0c:p0c + pw])\
                                .then_inc(LD.h, 16)
                        block.sync(_y)
                        ld_t = LD.n

                        def _v(e, ld_t=ld_t):
                            e.wait_ge(LD.h, ld_t)
                        block.vector(_v)
                        offs = [(0, 512), (512, 512)] if pw == 1024 else \
                            ([(0, 512), (512, 256)] if pw == 768 else [(0, pw)])
                        for so_, sw in offs:
                            quant(scrB[0:rr, so_:so_ + sw], u2[0:rr, so_:so_ + sw],
                                  sw, 1024, pp=rr)
                        ve_q = VE.n
                        OUTW = 16
                        LD.n += 16

                        def _y2(e, r0=r0, rr=rr, p0c=p0c, pw=pw, ve_q=ve_q,
                                cc_in=cc_in):
                            e.wait_ge(VE.h, ve_q)
                            e.dma_start(cc_in[r0:r0 + rr, p0c:p0c + pw],
                                        u2[0:rr, 0:pw]).then_inc(LD.h, 16)
                        block.sync(_y2)

            qslice(d["f1_wg_sl"], IFF // 8, C, ccg_in)
            qslice(d["f1_wu_sl"], IFF // 8, C, ccu_in)
            qslice(d["f1_wd_sl"], C // 8, IFF, ccd_in)
            ld_all = LD.n

            def _vwait(e, ld_all=ld_all):
                e.wait_ge(LD.h, ld_all)
            block.vector(_vwait)
            block.scalar(_vwait)
            CC.n += 3

            def _g(e, ld_all=ld_all):
                e.wait_ge(LD.h, ld_all)
                rg = [[0, 1, 2, 3, 4, 5, 6, 7]]
                e.collective_compute("AllGather", ALU.bypass, replica_groups=rg,
                                     ins=[ccg_in], outs=[ccg]).then_inc(CC.h, 1)
                e.collective_compute("AllGather", ALU.bypass, replica_groups=rg,
                                     ins=[ccu_in], outs=[ccu]).then_inc(CC.h, 1)
                e.collective_compute("AllGather", ALU.bypass, replica_groups=rg,
                                     ins=[ccd_in], outs=[ccd]).then_inc(CC.h, 1)
            block.gpsimd(_g)
            bar()

        def attention(layer):
            l = f"l{layer}"
            causal = (layer == 0)
            trw = (layer == 0)
            lnTb = u1[:].rearrange("p (c q) -> p c q", q=QCW)

            if layer == 1:
                GL.n += 32
                gl_b = GL.n

                def _g(e, gl_b=gl_b):
                    e.dma_start(bq_c[:], d["l1_bq"].rearrange("(c p) -> p c", p=128), allow_slow_non_contiguous=True).then_inc(GL.h, 16)
                    e.dma_start(bk_c[:], d["l1_bk"].rearrange("(c p) -> p c", p=128), allow_slow_non_contiguous=True).then_inc(GL.h, 16)
                    e.wait_ge(GL.h, gl_b)
                block.gpsimd(_g)
                bar()

            va3 = Va[:].rearrange("p (t c) -> p t c", c=C)
            prods = ("k", "v", "q")[:ATT_SUB]
            for which in prods:
                if which == "v":
                    load_brow(f"{l}_bv")
                if trw:
                    wprep_T(d[f"{l}_w{which}"], lambda cc: wcur[:, cc, :])
                else:
                    wload_nat(d[f"{l}_w{which}"], wcur)
                bar()
                for qc in range(NQC):
                    for tt in range(4):
                        t = qc * 4 + tt
                        ln_tile(t, u2[:, :])
                        transpose_tile(u2[:, :],
                                       lambda cc, tt=tt: lnTb[:, cc, tt * 128:(tt + 1) * 128])
                    bar()
                    if which in ("k", "q"):
                        dst = kTa[:].rearrange("p (c q) -> p c q", q=T) if which == "k" \
                            else qTa[:, :, :]
                        bcol = bk_c if which == "k" else bq_c
                        evac_tgts = []
                        for m in range(8):
                            t0 = TE.n
                            se_need = evac_tgts[m - 2] if m >= 2 else SE.n

                            def _t(e, m=m, se_need=se_need):
                                e.wait_ge(SE.h, se_need)
                                li = None
                                for cc in range(8):
                                    li = e.matmul(ps_s[m % 2][:, :],
                                                  wcur[:, cc, m * 128:(m + 1) * 128],
                                                  lnTb[:, cc, :],
                                                  start=(cc == 0), stop=(cc == 7))
                                li.then_inc(TE.h, 1)
                            block.tensor(_t)
                            TE.n += 1

                            def _s(e, m=m, qc=qc, t0=t0, dst=dst, bcol=bcol):
                                e.wait_ge(TE.h, t0 + 1)
                                e.activation(dst[:, m, qc * QCW:(qc + 1) * QCW],
                                             ps_s[m % 2][:, :], ACT.Identity,
                                             bias=bcol[:, m:m + 1])
                                e.drain()
                                e.nop().then_inc(SE.h, 1)
                            block.scalar(_s)
                            SE.n += 1
                            evac_tgts.append(SE.n)
                    else:
                        evac_tgts = []
                        for tt in range(4):
                            t = qc * 4 + tt
                            t0 = TE.n
                            ve_need = evac_tgts[tt - 1] if tt >= 1 else VE.n

                            def _t(e, tt=tt, ve_need=ve_need):
                                e.wait_ge(VE.h, ve_need)
                                li = None
                                for half in range(2):
                                    for cc in range(8):
                                        e.matmul(ps_w[:, half * 512:(half + 1) * 512],
                                                 lnTb[:, cc, tt * 128:(tt + 1) * 128],
                                                 wcur[:, cc, half * 512:(half + 1) * 512],
                                                 start=(cc == 0), stop=False)
                                    li = e.matmul(ps_w[:, half * 512:(half + 1) * 512],
                                                  tri[0:1, :],
                                                  brow[:, half * 512:(half + 1) * 512],
                                                  start=False, stop=True)
                                li.then_inc(TE.h, 1)
                            block.tensor(_t)
                            TE.n += 1

                            def _v(e, t=t, t0=t0):
                                e.wait_ge(TE.h, t0 + 1)
                                e.tensor_copy(va3[:, t, :], ps_w[:, :])
                                e.drain()
                                e.nop().then_inc(VE.h, 1)
                            block.vector(_v)
                            VE.n += 1
                            evac_tgts.append(VE.n)
                bar()

            if ATT_SUB < 4:
                return
            load_brow(f"{l}_bo")
            if trw:
                wprep_T(d[f"{l}_wo"], lambda cc: wcur[:, cc, :])
            else:
                wload_nat(d[f"{l}_wo"], wcur)
            bar()

            kT3 = kTa[:].rearrange("p (c q) -> p c q", q=T)
            cq = ctxq[:].rearrange("p (c q) -> p c q", q=QCW)
            rec_bf = scrB[32:48, 1024:1280].bitcast(BF16)
            alibi = u1[:, 3072:3584].bitcast(F32)
            if layer == 1:
                LD.n += 16
                al_t = LD.n

                def _yal(e):
                    e.dma_start(alibi, d["c_alibi"]).then_inc(LD.h, 16)
                block.sync(_yal)

                def _sal(e, al_t=al_t):
                    e.wait_ge(LD.h, al_t)
                block.scalar(_sal)
            for qc in range(NQC):
                kts = list(range(4 * qc + 4)) if causal else list(range(NKT))
                exp_done = {}
                mask_done = {}
                av_done = {}
                ctx_evac = {}
                den_evac = {}
                den_dma = {}
                for hi, h in enumerate(range(NH)):
                    p0_ = (h % 2) * 64
                    chk = h // 2
                    xpar = h % 2
                    dps = ps_e if (h % 2 == 0) else ps_w
                    for ki, kt in enumerate(kts):
                        spar = ki % 2
                        seqi = hi * len(kts) + ki
                        w_exp = exp_done.get(seqi - 2, 0)

                        def _t(e, h=h, kt=kt, spar=spar, p0_=p0_, chk=chk,
                               qc=qc, w_exp=w_exp):
                            e.wait_ge(SE.h, w_exp)
                            li = e.matmul(ps_s[spar][:, :],
                                          kT3[p0_:p0_ + 64, chk, kt * 128:(kt + 1) * 128],
                                          qTa[p0_:p0_ + 64, chk, qc * QCW:(qc + 1) * QCW],
                                          start=True, stop=True)
                            li.then_inc(TE.h, 1)
                        block.tensor(_t)
                        TE.n += 1
                        sc_t = TE.n
                        w_av2 = av_done.get(seqi - 2, 0)

                        def _s(e, h=h, kt=kt, spar=spar, sc_t=sc_t,
                               layer=layer, w_av2=w_av2):
                            e.wait_ge(TE.h, max(sc_t, w_av2))
                            if layer == 1:
                                b = alibi[:, h * NKT + kt:h * NKT + kt + 1]
                                e.activation(u2[:, spar * 512:(spar + 1) * 512],
                                             ps_s[spar][:, :], ACT.Exp,
                                             bias=b, scale=0.125)
                            else:
                                e.activation(u2[:, spar * 512:(spar + 1) * 512],
                                             ps_s[spar][:, :], ACT.Exp,
                                             scale=0.125)
                            e.drain()
                            e.nop().then_inc(SE.h, 1)
                        block.scalar(_s)
                        SE.n += 1
                        exp_done[seqi] = SE.n

                        need_mask = causal and (4 * qc <= kt <= 4 * qc + 3)
                        if need_mask:
                            v = kt - 4 * qc
                            se_t = SE.n

                            def _v(e, spar=spar, v=v, se_t=se_t):
                                e.wait_ge(SE.h, se_t)
                                if v > 0:
                                    e.memset(u2[:, spar * 512:spar * 512 + 128 * v], 0.0)
                                e.tensor_tensor(
                                    u2[:, spar * 512 + 128 * v:spar * 512 + 128 * (v + 1)],
                                    u2[:, spar * 512 + 128 * v:spar * 512 + 128 * (v + 1)],
                                    tri[:], ALU.mult)
                                e.drain()
                                e.nop().then_inc(VE.h, 1)
                            block.vector(_v)
                            VE.n += 1
                            mask_done[seqi] = VE.n

                        se_t = SE.n
                        ve_t = mask_done.get(seqi, None)
                        ve_ctx = ctx_evac.get(hi - 2, 0)
                        ve_den = den_evac.get(hi - 2, 0)

                        def _t2(e, h=h, kt=kt, spar=spar, xpar=xpar, p0_=p0_,
                                se_t=se_t, ve_t=ve_t, ve_ctx=ve_ctx,
                                ve_den=ve_den, dps=dps,
                                first=(ki == 0), last=(ki == len(kts) - 1)):
                            e.wait_ge(SE.h, se_t)
                            if ve_t is not None:
                                e.wait_ge(VE.h, ve_t)
                            if first:
                                e.wait_ge(VE.h, max(ve_ctx, ve_den))
                            e.matmul(ps_x[xpar][p0_:p0_ + 64, :],
                                     va3[:, kt, h * 64:(h + 1) * 64],
                                     u2[:, spar * 512:(spar + 1) * 512],
                                     start=first, stop=last)
                            li = e.matmul(dps[0:1, 0:512],
                                          tri[:, 127:128],
                                          u2[:, spar * 512:(spar + 1) * 512],
                                          start=first, stop=last)
                            li.then_inc(TE.h, 1)
                        block.tensor(_t2)
                        TE.n += 1
                        av_done[seqi] = TE.n

                    av_t = TE.n
                    dn_prev = den_dma.get(hi - 2, 0)

                    def _v(e, h=h, xpar=xpar, p0_=p0_, chk=chk, av_t=av_t,
                           dn_prev=dn_prev, dps=dps):
                        e.wait_ge(TE.h, av_t)
                        e.wait_ge(DN.h, dn_prev)
                        e.tensor_copy(cq[p0_:p0_ + 64, chk, :],
                                      ps_x[xpar][p0_:p0_ + 64, :])
                        e.drain()
                        e.nop().then_inc(VE.h, 1)
                        e.tensor_copy(scrB[0:1, xpar * 512:(xpar + 1) * 512],
                                      dps[0:1, 0:512])
                        e.drain()
                        e.nop().then_inc(VE.h, 1)
                    block.vector(_v)
                    VE.n += 1
                    VE.n += 1
                    ctx_evac[hi] = VE.n
                    den_evac[hi] = VE.n
                    ve_t = VE.n

                    def _y(e, h=h, xpar=xpar, ve_t=ve_t):
                        e.wait_ge(VE.h, ve_t)
                        e.dma_start(den16[h:h + 1, :],
                                    scrB[0:1, xpar * 512:(xpar + 1) * 512])\
                            .then_inc(DN.h, 16)
                    block.sync(_y)
                    DN.n += 16
                    den_dma[hi] = DN.n

                dn_t = DN.n

                def _vr(e, dn_t=dn_t):
                    e.wait_ge(DN.h, dn_t)
                    e.reciprocal(den16[0:16, :], den16[0:16, :])
                    e.drain()
                    e.tensor_copy(rec_bf, den16[0:16, :])
                    e.drain()
                    e.nop().then_inc(VE.h, 1)
                block.vector(_vr)
                VE.n += 1
                ve_rec = VE.n

                def _yrec(e, ve_rec=ve_rec):
                    e.wait_ge(VE.h, ve_rec)
                    e.dma_start(rec_dram, rec_bf).then_inc(DN.h, 16)
                block.sync(_yrec)
                DN.n += 16

                for chk in range(8):
                    ve_t = VE.n
                    rec_t = DN.n

                    def _y(e, chk=chk, ve_t=ve_t, rec_t=rec_t):
                        e.wait_ge(VE.h, ve_t)
                        e.wait_ge(DN.h, rec_t)
                        s0 = rec_dram[2 * chk, :].unsqueeze(0).broadcast_to([64, 512])
                        s1 = rec_dram[2 * chk + 1, :].unsqueeze(0).broadcast_to([64, 512])
                        e.dma_start(recb[0:64, :], s0).then_inc(DN.h, 16)
                        e.dma_start(recb[64:128, :], s1).then_inc(DN.h, 16)
                    block.sync(_y)
                    DN.n += 32
                    dn_t = DN.n

                    def _v2(e, chk=chk, dn_t=dn_t):
                        e.wait_ge(DN.h, dn_t)
                        e.tensor_tensor(cq[:, chk, :], cq[:, chk, :], recb[:, :],
                                        ALU.mult)
                        e.drain()
                        e.nop().then_inc(VE.h, 1)
                    block.vector(_v2)
                    VE.n += 1
                norm_all = VE.n

                for half in range(2 if ATT_SUB >= 5 else 0):
                    ve_free = VE.n
                    t0 = TE.n

                    def _t3(e, half=half, qc=qc, ve_free=ve_free):
                        e.wait_ge(VE.h, ve_free)
                        for ti in range(2):
                            t_loc = half * 2 + ti
                            for nh2 in range(2):
                                pp = (ps_s if ti == 0 else ps_x)[nh2]
                                for jc in range(8):
                                    e.matmul(pp[:, :],
                                             cq[:, jc, t_loc * 128:(t_loc + 1) * 128],
                                             wcur[:, jc, nh2 * 512:(nh2 + 1) * 512],
                                             start=(jc == 0), stop=False)
                                li = e.matmul(pp[:, :], tri[0:1, :],
                                              brow[:, nh2 * 512:(nh2 + 1) * 512],
                                              start=False, stop=True)
                        li.then_inc(TE.h, 1)
                    block.tensor(_t3)
                    TE.n += 1
                    op_t = TE.n

                    def _v3(e, half=half, qc=qc, op_t=op_t):
                        e.wait_ge(TE.h, op_t)
                        for ti in range(2):
                            t = qc * 4 + half * 2 + ti
                            for nh2 in range(2):
                                pp = (ps_s if ti == 0 else ps_x)[nh2]
                                e.tensor_tensor(xbuf[:, t, nh2 * 512:(nh2 + 1) * 512],
                                                xbuf[:, t, nh2 * 512:(nh2 + 1) * 512],
                                                pp[:, :], ALU.add)
                        e.drain()
                        e.nop().then_inc(VE.h, 1)
                    block.vector(_v3)
                    VE.n += 1
                bar()

        def ffn0():
            kf = kTa[:]
            xr = kf[:, 0:4096].rearrange("p (c q) -> p c q", q=512)
            xk = kf[:, 4096:8192].rearrange("p (c q) -> p c q", q=512)
            kact = kf[:, 8192:12288].rearrange("p (c q) -> p c q", q=512)
            Vf = Va[:]
            wkf = Vf[:, 0:8192].rearrange("p (c k) -> p c k", k=C)
            wvf = Vf[:, 8192:16384].rearrange("p (c k) -> p c k", k=C)
            ln2T = qTa

            wprep_T(d["f0_wr"], lambda cc: wcur[:, cc, :])
            wprep_T(d["f0_wk"], lambda cc: wkf[:, cc, :])
            wprep_T(d["f0_wv"], lambda cc: wvf[:, cc, :])
            bar()
            for t in range(NT):
                ln_tile(t, u2[:, :])
                transpose_tile(u2[:, :],
                               lambda cc, t=t: ln2T[:, cc, t * 128:(t + 1) * 128])
            bar()

            dbuf = scrB[:, 0:512]
            rfull = scrB[:, 512:1536]
            for pc in range(NQC):
                s0 = pc * 512
                for cc in range(8):
                    def _v(e, cc=cc, s0=s0, pc=pc):
                        if pc == 0:
                            cur = ln2T[:, cc, 1:512]
                            prv = ln2T[:, cc, 0:511]
                            e.tensor_sub(dbuf[:, 1:512], cur, prv)
                            e.drain()
                            e.scalar_tensor_tensor(xr[:, cc, 1:512], dbuf[:, 1:512],
                                                   mur_c[:, cc:cc + 1], prv,
                                                   ALU.mult, ALU.add)
                            e.scalar_tensor_tensor(xk[:, cc, 1:512], dbuf[:, 1:512],
                                                   muk_c[:, cc:cc + 1], prv,
                                                   ALU.mult, ALU.add)
                            e.drain()
                            e.tensor_scalar(xr[:, cc, 0:1], ln2T[:, cc, 0:1],
                                            mur_c[:, cc:cc + 1], None, ALU.mult)
                            e.tensor_scalar(xk[:, cc, 0:1], ln2T[:, cc, 0:1],
                                            muk_c[:, cc:cc + 1], None, ALU.mult)
                            e.drain()
                        else:
                            cur = ln2T[:, cc, s0:s0 + 512]
                            prv = ln2T[:, cc, s0 - 1:s0 + 511]
                            e.tensor_sub(dbuf[:, :], cur, prv)
                            e.drain()
                            e.scalar_tensor_tensor(xr[:, cc, :], dbuf[:, :],
                                                   mur_c[:, cc:cc + 1], prv,
                                                   ALU.mult, ALU.add)
                            e.scalar_tensor_tensor(xk[:, cc, :], dbuf[:, :],
                                                   muk_c[:, cc:cc + 1], prv,
                                                   ALU.mult, ALU.add)
                            e.drain()
                        e.nop().then_inc(VE.h, 1)
                    block.vector(_v)
                    VE.n += 1
                xrk_done = VE.n

                se_evt = []
                for m in range(8):
                    t0 = TE.n
                    se_prev = se_evt[m - 2] if m >= 2 else SE.n

                    def _t(e, m=m, se_prev=se_prev, xrk=xrk_done):
                        e.wait_ge(VE.h, xrk)
                        e.wait_ge(SE.h, se_prev)
                        li = None
                        for cc in range(8):
                            li = e.matmul(ps_s[m % 2][:, :],
                                          wkf[:, cc, m * 128:(m + 1) * 128],
                                          xk[:, cc, :], start=(cc == 0), stop=(cc == 7))
                        li.then_inc(TE.h, 1)
                    block.tensor(_t)
                    TE.n += 1

                    def _s(e, m=m, t0=t0):
                        e.wait_ge(TE.h, t0 + 1)
                        e.activation(u2[:, (m % 2) * 512:(m % 2) * 512 + 512],
                                     ps_s[m % 2][:, :], ACT.Relu)
                        e.drain()
                        e.nop().then_inc(SE.h, 1)
                    block.scalar(_s)
                    SE.n += 1
                    se_evt.append(SE.n)
                    se_now = SE.n

                    def _v(e, m=m, se_now=se_now):
                        e.wait_ge(SE.h, se_now)
                        rk = u2[:, (m % 2) * 512:(m % 2) * 512 + 512]
                        e.tensor_tensor(kact[:, m, :], rk, rk, ALU.mult)
                        e.drain()
                        e.nop().then_inc(VE.h, 1)
                    block.vector(_v)
                    VE.n += 1
                kact_done = VE.n

                for tt in range(4):
                    t = pc * 4 + tt
                    t0 = TE.n
                    ve_now = VE.n

                    def _t(e, tt=tt, kd=kact_done, ve_now=ve_now):
                        e.wait_ge(VE.h, max(kd, ve_now))
                        li = None
                        for half in range(2):
                            for cc in range(8):
                                li = e.matmul(ps_w[:, half * 512:(half + 1) * 512],
                                              xr[:, cc, tt * 128:(tt + 1) * 128],
                                              wcur[:, cc, half * 512:(half + 1) * 512],
                                              start=(cc == 0), stop=(cc == 7))
                        li.then_inc(TE.h, 1)
                    block.tensor(_t)
                    TE.n += 1
                    r_t = TE.n

                    def _s(e, r_t=r_t):
                        e.wait_ge(TE.h, r_t)
                        e.activation(rfull[:, :], ps_w[:, :], ACT.Sigmoid)
                        e.drain()
                        e.nop().then_inc(SE.h, 1)
                    block.scalar(_s)
                    SE.n += 1
                    sg_t = SE.n

                    def _t2(e, tt=tt, sg_t=sg_t):
                        e.wait_ge(SE.h, sg_t)
                        li = None
                        for half in range(2):
                            for m in range(8):
                                li = e.matmul(ps_w[:, half * 512:(half + 1) * 512],
                                              kact[:, m, tt * 128:(tt + 1) * 128],
                                              wvf[:, m, half * 512:(half + 1) * 512],
                                              start=(m == 0), stop=(m == 7))
                        li.then_inc(TE.h, 1)
                    block.tensor(_t2)
                    TE.n += 1
                    ov_t = TE.n

                    def _v(e, t=t, ov_t=ov_t):
                        e.wait_ge(TE.h, ov_t)
                        e.tensor_tensor(rfull[:, :], rfull[:, :], ps_w[:, :],
                                        ALU.mult)
                        e.drain()
                        e.tensor_tensor(xbuf[:, t, :], xbuf[:, t, :], rfull[:, :],
                                        ALU.add)
                        e.drain()
                        e.nop().then_inc(VE.h, 1)
                    block.vector(_v)
                    VE.n += 1
            bar()

        def ffn1():
            xqT = qTa
            kf = kTa[:]
            for t in range(NT):
                ln_tile(t, scrB[:, 0:1024])
                ve_t = VE.n
                se_t = SE.n

                def _v0(e, se_t=se_t):
                    e.wait_ge(SE.h, se_t)
                block.vector(_v0)
                for sub in range(2):
                    quant(scrB[:, sub * 512:(sub + 1) * 512],
                          u2[:, sub * 512:(sub + 1) * 512], 512, 1024)
                transpose_tile(u2[:, :],
                               lambda cc, t=t: xqT[:, cc, t * 128:(t + 1) * 128])
            bar()

            wg_sl = [kf[:, s * 2048:(s + 1) * 2048].rearrange("p (c k) -> p c k", k=256)
                     for s in range(2)]
            wu_sl = [kf[:, (2 + s) * 2048:(3 + s) * 2048].rearrange("p (c k) -> p c k", k=256)
                     for s in range(2)]
            hqw = [kf[:, 8192 + s * 1024:8192 + (s + 1) * 1024] for s in range(2)]
            hbuf = scrB[:, 0:256]

            wq_g = Va[:, 2048:3072]
            cc_t = CC.n

            def prep_wslice(ccsrc, dstsl, ic, te_guard):
                for rt in range(2):
                    ve_t = VE.n
                    LD.n += 16
                    ld_t = LD.n

                    def _y(e, rt=rt, ic=ic, ve_t=ve_t, ccsrc=ccsrc, cc_t=cc_t):
                        e.wait_ge(CC.h, cc_t)
                        e.wait_ge(VE.h, ve_t)
                        e.dma_start(wq_g[:, :],
                                    ccsrc[ic * 256 + rt * 128:ic * 256 + (rt + 1) * 128, :])\
                            .then_inc(LD.h, 16)
                    block.sync(_y)
                    VE.n += 1

                    def _v0(e, ld_t=ld_t):
                        e.wait_ge(LD.h, ld_t)
                        e.nop().then_inc(VE.h, 1)
                    block.vector(_v0)
                    transpose_tile(wq_g[:, :],
                                   lambda cc, rt=rt: dstsl[:, cc, rt * 128:(rt + 1) * 128],
                                   te_guard=te_guard)
            NIC = IFF // 256
            hq_dmas = {}
            slot_use = {}
            for ic in range(NIC):
                sl = ic % 2
                prep_wslice(ccg, wg_sl[sl], ic, slot_use.get(ic - 2, 0))
                prep_wslice(ccu, wu_sl[sl], ic, slot_use.get(ic - 2, 0))
                block.flush()
                for t in range(NT):
                    t0 = TE.n
                    se_t = SE.n
                    ve_t = VE.n

                    def _t(e, t=t, sl=sl, se_t=se_t, ve_t=ve_t):
                        e.wait_ge(SE.h, se_t)
                        e.wait_ge(VE.h, ve_t)
                        li = None
                        for cc in range(8):
                            e.matmul(ps_s[0][:, 0:256],
                                     xqT[:, cc, t * 128:(t + 1) * 128],
                                     wg_sl[sl][:, cc, :], start=(cc == 0), stop=(cc == 7))
                        for cc in range(8):
                            li = e.matmul(ps_s[1][:, 0:256],
                                          xqT[:, cc, t * 128:(t + 1) * 128],
                                          wu_sl[sl][:, cc, :], start=(cc == 0), stop=(cc == 7))
                        li.then_inc(TE.h, 1)
                    block.tensor(_t)
                    TE.n += 1
                    mm_t = TE.n

                    def _s(e, mm_t=mm_t):
                        e.wait_ge(TE.h, mm_t)
                        e.activation(hbuf[:, :], ps_s[0][:, 0:256], ACT.Silu)
                        e.drain()
                        e.nop().then_inc(SE.h, 1)
                    block.scalar(_s)
                    SE.n += 1
                    si_t = SE.n

                    def _v(e, si_t=si_t):
                        e.wait_ge(SE.h, si_t)
                        e.tensor_tensor(hbuf[:, :], hbuf[:, :], ps_s[1][:, 0:256],
                                        ALU.mult)
                        e.drain()
                        e.nop().then_inc(VE.h, 1)
                    block.vector(_v)
                    VE.n += 1
                    quant(hbuf[:, :], u2[:, 0:256], 256, 1024)
                    hs = t % 2
                    hq_prev = hq_dmas.get(t - 2, 0)

                    def _vw(e, hq_prev=hq_prev):
                        e.wait_ge(HQ.h, hq_prev)
                    block.vector(_vw)
                    transpose_tile(u2[:, 0:256],
                                   lambda b, hs=hs: hqw[hs][:, b * 128:(b + 1) * 128],
                                   ncc=2)
                    ve_t2 = VE.n

                    def _y(e, t=t, ic=ic, hs=hs, ve_t2=ve_t2):
                        e.wait_ge(VE.h, ve_t2)
                        e.dma_start(hq_dram[t, ic * 2:(ic + 1) * 2].rearrange("b p q -> p b q"),
                                    hqw[hs][:, 0:256].rearrange("p (b q) -> p b q", q=128))\
                            .then_inc(HQ.h, 16)
                    block.sync(_y)
                    HQ.n += 16
                    hq_dmas[t] = HQ.n
                slot_use[ic] = TE.n
                block.flush()
            bar()

            wd_chunk = []
            Vf = Va[:]
            cqf = ctxq[:]
            u1f = u1[:]
            for kk in range(NIT):
                if kk < 16:
                    wd_chunk.append(Vf[:, kk * 1024:(kk + 1) * 1024])
                elif kk < 20:
                    wd_chunk.append(cqf[:, (kk - 16) * 1024:(kk - 15) * 1024])
                else:
                    wd_chunk.append(u1f[:, (kk - 20) * 1024:(kk - 19) * 1024])

            for rt in range(8):
                for piece, pw in ((0, 1024), (1, 1024), (2, 768)):
                    off = piece * 1024
                    ve_t = VE.n
                    LD.n += 16
                    ld_t = LD.n

                    def _y(e, rt=rt, off=off, pw=pw, ve_t=ve_t, cc_t=cc_t):
                        e.wait_ge(CC.h, cc_t)
                        e.wait_ge(VE.h, ve_t)
                        e.dma_start(u2[:, 0:pw],
                                    ccd[rt * 128:(rt + 1) * 128, off:off + pw])\
                            .then_inc(LD.h, 16)
                    block.sync(_y)
                    VE.n += 1

                    def _v(e, ld_t=ld_t):
                        e.wait_ge(LD.h, ld_t)
                        e.nop().then_inc(VE.h, 1)
                    block.vector(_v)
                    nb128 = pw // 128
                    transpose_tile(u2[:, 0:pw],
                                   lambda b, rt=rt, off=off: wd_chunk[(off // 128) + b][:, rt * 128:(rt + 1) * 128],
                                   ncc=nb128)
            bar()

            hq_sl = [kf[:, s * 2816:(s + 1) * 2816] for s in range(2)]
            use_done = {}
            for t in range(NT):
                sl = t % 2
                prev_use = use_done.get(t - 2, 0)

                def _y(e, t=t, sl=sl, prev_use=prev_use):
                    e.wait_ge(TE.h, prev_use)
                    e.dma_start(hq_sl[sl][:].rearrange("p (b q) -> p b q", q=128),
                                hq_dram[t].rearrange("b p q -> p b q")).then_inc(HQ.h, 16)
                block.sync(_y)
                HQ.n += 16
                hq_t = HQ.n
                ve_t = VE.n

                def _t(e, t=t, sl=sl, hq_t=hq_t, ve_t=ve_t):
                    e.wait_ge(HQ.h, hq_t)
                    e.wait_ge(VE.h, ve_t)
                    li = None
                    for half in range(2):
                        for kk in range(NIT):
                            li = e.matmul(ps_w[:, half * 512:(half + 1) * 512],
                                          hq_sl[sl][:, kk * 128:(kk + 1) * 128],
                                          wd_chunk[kk][:, half * 512:(half + 1) * 512],
                                          start=(kk == 0), stop=(kk == NIT - 1))
                    li.then_inc(TE.h, 1)
                block.tensor(_t)
                TE.n += 1
                use_done[t] = TE.n
                mm_t = TE.n

                def _v(e, t=t, mm_t=mm_t):
                    e.wait_ge(TE.h, mm_t)
                    e.tensor_tensor(xbuf[:, t, :], xbuf[:, t, :], ps_w[:, :],
                                    ALU.add)
                    e.drain()
                    e.nop().then_inc(VE.h, 1)
                block.vector(_v)
                VE.n += 1
            bar()

        p0()
        if stage >= 4:
            ffn1_weights_cc()
        if stage >= 1:
            attention(0)
        if stage >= 2:
            ffn0()
        if stage >= 3:
            attention(1)
        if stage >= 4:
            ffn1()
        emit_out()

    return nc


def _consts():
    ident = np.eye(128, dtype=np.float32).astype(ml_dtypes.bfloat16)
    kp = np.arange(128)[:, None]
    qp = np.arange(128)[None, :]
    tri = (qp >= kp).astype(np.float32).astype(ml_dtypes.bfloat16)
    alibi = np.zeros((128, NH * NKT), dtype=np.float32)
    xs = (2.0 ** 8) ** (1.0 / NH)
    slopes = np.array([1.0 / xs ** (i + 1) for i in range(NH)], dtype=np.float64)
    p = np.arange(128)
    for h in range(NH):
        for kt in range(NKT):
            alibi[:, h * NKT + kt] = (slopes[h] * (kt * 128 + p - (T - 1))).astype(np.float32)
    return ident, tri, alibi


def kernel(**inputs):
    B = 8
    ident, tri, alibi = _consts()
    nc = _build(STAGE)
    base = {
        "l0_wq": inputs["L0_attn_Wq"], "l0_wk": inputs["L0_attn_Wk"],
        "l0_wv": inputs["L0_attn_Wv"], "l0_wo": inputs["L0_attn_Wo"],
        "l0_bq": inputs["L0_attn_bq"], "l0_bk": inputs["L0_attn_bk"],
        "l0_bv": inputs["L0_attn_bv"], "l0_bo": inputs["L0_attn_bo"],
        "f0_wr": inputs["L0_ffn_W_r"], "f0_wk": inputs["L0_ffn_W_k"],
        "f0_wv": inputs["L0_ffn_W_v"],
        "f0_mur": inputs["L0_ffn_mu_r"], "f0_muk": inputs["L0_ffn_mu_k"],
        "l1_wq": inputs["L1_attn_Wq"], "l1_wk": inputs["L1_attn_Wk"],
        "l1_wv": inputs["L1_attn_Wv"], "l1_wo": inputs["L1_attn_Wo"],
        "l1_bq": inputs["L1_attn_bq"], "l1_bk": inputs["L1_attn_bk"],
        "l1_bv": inputs["L1_attn_bv"], "l1_bo": inputs["L1_attn_bo"],
        "c_ident": ident, "c_tri": tri, "c_alibi": alibi,
    }
    base = {k: np.ascontiguousarray(np.asarray(v)) for k, v in base.items()}
    wg = np.asarray(inputs["L1_ffn_w_gate"])
    wu = np.asarray(inputs["L1_ffn_w_up"])
    wd = np.asarray(inputs["L1_ffn_w_down"])
    in_maps = []
    for b in range(B):
        m = dict(base)
        m["x"] = np.ascontiguousarray(np.asarray(inputs["L0_attn_x"][b]))
        m["f1_wg_sl"] = np.ascontiguousarray(wg[b * (IFF // 8):(b + 1) * (IFF // 8)])
        m["f1_wu_sl"] = np.ascontiguousarray(wu[b * (IFF // 8):(b + 1) * (IFF // 8)])
        m["f1_wd_sl"] = np.ascontiguousarray(wd[b * (C // 8):(b + 1) * (C // 8)])
        in_maps.append(m)
    res = run_bass_kernel_spmd(nc, in_maps, core_ids=list(range(B)))
    out = np.stack([np.asarray(res.results[b]["out"]) for b in range(B)], axis=0)
    return out.astype(np.float32)
